# version 10
# baseline (speedup 1.0000x reference)
import numpy as np
import ml_dtypes
from contextlib import ExitStack

import concourse.bass as bass
import concourse.mybir as mybir
from concourse.bass_utils import run_bass_kernel_spmd

F32 = mybir.dt.float32
BF16 = mybir.dt.bfloat16
FP8 = mybir.dt.float8e4
U8 = mybir.dt.uint8
AF = mybir.ActivationFunctionType
ALU = mybir.AluOpType
AX = mybir.AxisListType

B, T, E = 4, 4096, 2048
H, Dh = 16, 128
I = 6144
NCORE = 8
TOK = 2048
NSLOT = 4
EXT = [8, 6, 4, 2]
CHUNKS = {0: [7, 5, 2, 0], 1: [6, 4, 3, 1]}
ET = E // 128
IT = I // 128
INV_SQRT_D = float(np.float32(1.0) / np.float32(np.sqrt(np.float32(Dh))))
ENGINES = ("sync", "gpsimd", "tensor", "vector", "scalar")
COMPUTE = ("tensor", "vector", "scalar")
DTS = {F32: 4, BF16: 2, U8: 1, FP8: 1}


def own_rows(half):
    return np.concatenate([np.arange(c * 512, (c + 1) * 512) for c in CHUNKS[half]])


_TABLES = None


def rope_tables():
    global _TABLES
    if _TABLES is None:
        f32 = np.float32
        e = np.arange(0, Dh, 2, dtype=f32) / f32(Dh)
        inv = (f32(1.0) / np.power(f32(1e6), e)).astype(f32)
        f = np.arange(T, dtype=f32)[:, None] * inv[None, :]
        cos = np.concatenate([np.cos(f), np.cos(f)], -1).astype(f32)
        sin = np.concatenate([np.sin(f), np.sin(f)], -1).astype(f32)
        _TABLES = (cos, sin)
    return _TABLES


class Prog:

    def __init__(self):
        self.ops = {e: [] for e in ENGINES}
        self.cnt = {k: 0 for k in ("ds", "dg", "tm", "vm", "sm", "out", "cs",
                                    "xl0", "xl1", "wg0", "wg1", "sh", "sp",
                                    "hk0", "hk1", "dr0", "dr1", "fb0", "fb1",
                                    "ou0", "ou1", "rg", "d8s", "wf0", "wf1")}
        self.bank = {}

    def wait(self, eng, sem, val):
        if val:
            self.ops[eng].append(lambda E, s=sem, v=val: E.wait_ge(self.sem[s], v))

    def wait_all(self, eng, sem):
        self.wait(eng, sem, self.cnt[sem])

    def dma(self, eng, out, in_, sem=None):
        if sem is None:
            sem = "ds" if eng == "sync" else "dg"
        self.cnt[sem] += 16
        v = self.cnt[sem]

        def f(E, out=out, in_=in_, s=sem):
            E.dma_start(out, in_).then_inc(self.sem[s], 16)

        self.ops[eng].append(f)
        return v

    def op(self, eng, fn, sem=None):
        v = None
        if sem is not None:
            self.cnt[sem] += 1
            v = self.cnt[sem]

        def f(E, fn=fn, s=sem):
            inst = fn(E)
            if s is not None:
                inst.then_inc(self.sem[s], 1)

        self.ops[eng].append(f)
        return v

    def bank_acquire(self, bk):
        g = self.bank.get(bk)
        if g:
            self.wait("tensor", g[0], g[1])
            self.bank[bk] = None

    def bank_release(self, bk, sem, val):
        self.bank[bk] = (sem, val)

    def barrier(self, engines=ENGINES):
        for e in engines:
            for s, v in list(self.cnt.items()):
                self.wait(e, s, v)

    def emit(self, nc, sem_handles):
        self.sem = sem_handles
        with nc.Block() as block:
            @block.sync
            def _(e):
                for f in self.ops["sync"]:
                    f(e)

            @block.gpsimd
            def _(e):
                for f in self.ops["gpsimd"]:
                    f(e)

            @block.tensor
            def _(e):
                for f in self.ops["tensor"]:
                    f(e)

            @block.vector
            def _(e):
                for f in self.ops["vector"]:
                    f(e)

            @block.scalar
            def _(e):
                for f in self.ops["scalar"]:
                    f(e)


def build_nc(phases=("p1", "p2", "po", "p3"), dbg=False):
    nc = bass.Bass("TRN2", target_bir_lowering=False,
                   detect_race_conditions=False)
    SCR = "ExternalOutput" if dbg else "Internal"

    xb = nc.dram_tensor("xb", [T, E], F32, kind="ExternalInput")
    xo = nc.dram_tensor("xo", [TOK, E], F32, kind="ExternalInput")
    wqT = nc.dram_tensor("wqT", [E, E], F32, kind="ExternalInput")
    wkT = nc.dram_tensor("wkT", [E, E], F32, kind="ExternalInput")
    wvT = nc.dram_tensor("wvT", [E, E], F32, kind="ExternalInput")
    woT = nc.dram_tensor("woT", [E, E], F32, kind="ExternalInput")
    gw = nc.dram_tensor("gw", [E, I], F32, kind="ExternalInput")
    uw = nc.dram_tensor("uw", [E, I], F32, kind="ExternalInput")
    dw = nc.dram_tensor("dw", [I, E], F32, kind="ExternalInput")
    gs = nc.dram_tensor("gs", [I], F32, kind="ExternalInput")
    us = nc.dram_tensor("us", [I], F32, kind="ExternalInput")
    ds_ = nc.dram_tensor("ds", [E], F32, kind="ExternalInput")
    cosk = nc.dram_tensor("cosk", [128, T], F32, kind="ExternalInput")
    sink = nc.dram_tensor("sink", [128, T], F32, kind="ExternalInput")
    cosq = nc.dram_tensor("cosq", [128, TOK], F32, kind="ExternalInput")
    sinq = nc.dram_tensor("sinq", [128, TOK], F32, kind="ExternalInput")
    identb = nc.dram_tensor("identb", [128, 128], BF16, kind="ExternalInput")
    masks = nc.dram_tensor("masks", [128, NSLOT, 8, 512], U8, kind="ExternalInput")

    k_sc = nc.dram_tensor("k_sc", [H, 128, T], BF16, kind=SCR)
    v_sc = nc.dram_tensor("v_sc", [H, 128, T // 128, 128], BF16, kind=SCR)
    q_sc = nc.dram_tensor("q_sc", [H, 128, TOK], BF16, kind=SCR)
    xnt_sc = nc.dram_tensor("xnt_sc", [ET, 128, TOK], BF16, kind=SCR)
    fuse_sc = nc.dram_tensor("fuse_sc", [TOK // 128, 128, IT * 128], BF16, kind=SCR)
    r_sc = nc.dram_tensor("r_sc", [TOK, E], F32, kind=SCR)
    y = nc.dram_tensor("y", [TOK, E], F32, kind="ExternalOutput")

    es = ExitStack()
    SB = lambda nm, sh, dt: es.enter_context(nc.sbuf_tensor(nm, sh, dt))
    const_idb = SB("c_idb", [128, 128], BF16)
    negB = SB("negB", [128, H], F32)
    gs_sb = SB("gs_sb", [128, IT], F32)
    us_sb = SB("us_sb", [128, IT], F32)
    dsb = SB("dsb", [128, E], F32)
    ones_b = SB("ones_b", [128, 1], BF16)
    ones2 = SB("ones2", [128, 128], F32)
    nrm = SB("nrm", [1, 2 * H], F32)
    nbz = SB("nbz", [128, H], F32)

    pb = [es.enter_context(nc.psum_tensor(f"pb{i}", [128, 512], F32))
          for i in range(8)]
    pbb = [t.bitcast(BF16) for t in pb]

    AR = 189 * 1024
    arena = nc.alloc_sbuf_tensor("arena", [128, AR], mybir.dt.uint8)
    abase = nc.lookup_mloc(arena).addr
    off = [0]

    def A(nm, sh, dt):
        nbytes = int(np.prod(sh[1:])) * DTS[dt]
        o = (off[0] + 31) // 32 * 32
        t = nc.alloc_sbuf_tensor_at(nm, sh, dt, offset=abase + o)
        off[0] = o + nbytes
        assert off[0] <= AR, f"arena overflow at {nm}: {off[0]}"
        return t

    p = Prog()

    def mm(psum_ap, lhsT, rhs, start, stop, sem=None):
        return p.op("tensor",
                    lambda E, o=psum_ap, l=lhsT, r=rhs, a=start, b=stop:
                    E.matmul(o, l, r, start=a, stop=b), sem=sem)

    def transpose(psum_ap, in_ap, sem=None):
        return p.op("tensor",
                    lambda E, o=psum_ap, i=in_ap:
                    E.transpose(o, i, const_idb[:]), sem=sem)

    c_cosk = A("c_cosk", [128, T], F32)
    c_sink = A("c_sink", [128, T], F32)
    c_cosq = A("c_cosq", [128, TOK], F32)
    c_sinq = A("c_sinq", [128, TOK], F32)
    _off_tab = off[0]
    dz = A("dz", [128, E], F32)
    off[0] = _off_tab

    p.dma("sync", c_cosk[:], cosk[:], sem="cs")
    p.dma("sync", c_sink[:], sink[:], sem="cs")
    p.dma("sync", c_cosq[:], cosq[:], sem="cs")
    p.dma("sync", c_sinq[:], sinq[:], sem="cs")
    id_ld = p.dma("sync", const_idb[:], identb[:], sem="cs")
    for j in range(IT):
        p.dma("sync", gs_sb[:, j:j + 1], bass.AP(gs, j * 128, [[1, 128], [1, 1]]),
              sem="cs")
        p.dma("sync", us_sb[:, j:j + 1], bass.AP(us, j * 128, [[1, 128], [1, 1]]),
              sem="cs")
    p.op("vector", lambda E: E.memset(ones_b[:], 1.0))
    p.op("vector", lambda E: E.memset(ones2[:], 1.0))
    p.op("vector", lambda E: E.memset(nbz[:], 0.0))
    zd = p.op("vector", lambda E: E.memset(dz[:], 0.0), sem="vm")
    p.op("vector", lambda E: E.memset(nrm[:], 0.0), sem="vm")
    p.wait("sync", "vm", zd)
    p.dma("sync", dz[0:1, :], bass.AP(ds_, 0, [[E, 1], [1, E]]), sem="cs")
    p.wait_all("tensor", "cs")
    p.wait("tensor", "vm", zd)
    for j in range(4):
        p.bank_acquire(7)
        tv = mm(pb[7][:, :], ones2[:], dz[:, j * 512:(j + 1) * 512],
                True, True, sem="tm")
        p.wait("vector", "tm", tv)
        ev = p.op("vector",
                  lambda E, j=j: E.tensor_copy(dsb[:, j * 512:(j + 1) * 512],
                                               pb[7][:, :]),
                  sem="vm")
        p.bank_release(7, "vm", ev)

    xnt = [A(f"xnt{i}", [128, 2048], BF16) for i in range(ET)]
    x_sb = [A(f"xsb{i}", [128, E], F32) for i in range(2)]
    sqb = A("sqb", [128, E], F32)
    xnb = [A(f"xnb{i}", [128, E], BF16) for i in range(2)]
    w_sb = [A(f"wsb{i}", [128, E], BF16) for i in range(2)]
    _u0 = (off[0] + 31) // 32 * 32
    kcw = A("kcw", [128, 2048], F32)
    kww = A("kww", [128, 2048], F32)
    rshw = A("rshw", [128, 2048], F32)
    rbw = A("rbw", [128, 2048], BF16)
    sqw = [A(f"sqw{i}", [128, 2048], BF16) for i in range(2)]
    wv_sb = nc.alloc_sbuf_tensor_at("wvsb", [128, ET * 512], BF16,
                                    offset=abase + _u0)
    vst = [nc.alloc_sbuf_tensor_at(f"vst{i}", [128, 512], BF16,
                                   offset=abase + _u0 + 16 * 1024 + i * 1024)
           for i in range(4)]
    stat = [A(f"stat{i}", [128, 4], F32) for i in range(2)]
    assert off[0] <= AR
    E_ = float(E)

    xnt_spill = [0] * ET

    if "p1" in phases:
        ln_x_free = [0, 0]
        xnb_free = [0, 0]
        sq_free = [0]

        def ln_group(src, t0):
            p.wait_all("tensor", "cs")
            p.wait("sync", "sm", p.cnt["sm"])
            for tb in range(16):
                buf = tb % 2
                p.wait("sync", "sm", ln_x_free[buf])
                p.dma("sync", x_sb[buf][:],
                      src[t0 + tb * 128: t0 + (tb + 1) * 128, :],
                      sem="xl0" if buf == 0 else "xl1")
                p.wait_all("vector", "xl0" if buf == 0 else "xl1")
                r1 = p.op("vector", lambda E, b=buf:
                          E.reduce_sum(stat[b][:, 0:1], x_sb[b][:], axis=AX.X),
                          sem="vm")
                p.wait("vector", "vm", r1)
                r2 = p.op("vector", lambda E, b=buf: E.tensor_scalar(
                    stat[b][:, 1:2], stat[b][:, 0:1], 1.0 / E_, None, ALU.mult),
                    sem="vm")
                p.wait("vector", "vm", r2)
                vsub = p.op("vector", lambda E, b=buf: E.tensor_scalar(
                    x_sb[b][:], x_sb[b][:], stat[b][:, 1:2], None, ALU.subtract),
                    sem="vm")
                p.wait("scalar", "vm", vsub)
                p.wait("scalar", "vm", sq_free[0])
                ssq = p.op("scalar", lambda E, b=buf:
                           E.activation(sqb[:], x_sb[b][:], AF.Square), sem="sm")
                p.wait("vector", "sm", ssq)
                vq = p.op("vector", lambda E, b=buf:
                          E.reduce_sum(stat[b][:, 2:3], sqb[:], axis=AX.X), sem="vm")
                sq_free[0] = vq
                p.wait("vector", "vm", vq)
                vv_ = p.op("vector", lambda E, b=buf: E.tensor_scalar(
                    stat[b][:, 2:3], stat[b][:, 2:3], 1.0 / E_, 1e-5,
                    ALU.mult, ALU.add), sem="vm")
                p.wait("scalar", "vm", vv_)
                sq_ = p.op("scalar", lambda E, b=buf:
                           E.activation(stat[b][:, 3:4], stat[b][:, 2:3], AF.Sqrt),
                           sem="sm")
                p.wait("vector", "sm", sq_)
                vrec = p.op("vector", lambda E, b=buf:
                            E.reciprocal(stat[b][:, 3:4], stat[b][:, 3:4]), sem="vm")
                p.wait("scalar", "vm", vrec)
                p.wait("scalar", "tm", xnb_free[buf])
                sxn = p.op("scalar", lambda E, b=buf: E.activation(
                    xnb[b][:], x_sb[b][:], AF.Copy, scale=stat[b][:, 3:4]), sem="sm")
                ln_x_free[buf] = sxn
                p.wait("tensor", "sm", sxn)
                for et in range(ET):
                    bk = et % 2
                    p.bank_acquire(bk)
                    tv = transpose(pbb[bk][:, 0:128],
                                   xnb[buf][:, et * 128:(et + 1) * 128], sem="tm")
                    if et >= ET - 2:
                        xnb_free[buf] = tv
                    eng = "vector" if et % 2 else "scalar"
                    sem = "vm" if et % 2 else "sm"
                    p.wait(eng, "tm", tv)
                    if et % 2:
                        ev = p.op(eng, lambda E, et=et, tb=tb, bk=bk:
                                  E.tensor_copy(
                                      xnt[et][:, tb * 128:(tb + 1) * 128],
                                      pbb[bk][:, 0:128]), sem=sem)
                    else:
                        ev = p.op(eng, lambda E, et=et, tb=tb, bk=bk:
                                  E.copy(xnt[et][:, tb * 128:(tb + 1) * 128],
                                         pbb[bk][:, 0:128]), sem=sem)
                    p.bank_release(bk, sem, ev)

        def proj_kq(dst_sc, wT_dram, cos_sb, sin_sb, t0, name):
            p.wait("tensor", "vm", p.cnt["vm"])
            p.wait("tensor", "sm", p.cnt["sm"])
            p.wait("sync", "sp", p.cnt["sp"])
            nrm_off = 0 if name == "q" else H
            wbuf_free = [p.cnt["tm"], p.cnt["tm"]]
            rope_free = [0]
            spill_sq = [0, 0]
            norm_last = [0, 0]
            pend_norm = []
            wcast = [p.cnt["sm"], p.cnt["sm"]]
            for ft in range(H):
                wb = ft % 2
                wf = "wf0" if wb == 0 else "wf1"
                p.wait("sync", "sm", wcast[wb])
                p.dma("sync", x_sb[wb][:],
                      bass.AP(wT_dram, ft * 128,
                              [[E, 128], [128 * E, ET], [1, 128]]), sem=wf)
                p.wait_all("scalar", wf)
                p.wait("scalar", "tm", wbuf_free[wb])
                wc = p.op("scalar", lambda E, wb=wb:
                          E.copy(w_sb[wb][:], x_sb[wb][:]), sem="sm")
                wcast[wb] = wc
                p.wait("tensor", "sm", wc)
                last_mm_w = 0
                kwv = 0
                for tc in range(4):
                    bk = tc
                    p.bank_acquire(bk)
                    for et in range(ET):
                        v = mm(pb[bk][:, :], w_sb[wb][:, et * 128:(et + 1) * 128],
                               xnt[et][:, tc * 512:(tc + 1) * 512],
                               et == 0, et == ET - 1,
                               sem="tm" if et == ET - 1 else None)
                    last_mm_w = v
                    cs = slice(t0 + tc * 512, t0 + (tc + 1) * 512)
                    ws = slice(tc * 512, (tc + 1) * 512)
                    p.wait("vector", "tm", v)
                    if tc == 0:
                        p.wait("vector", "vm", rope_free[0])
                        p.wait_all("vector", "sp")
                        p.wait("vector", "sm", spill_sq[1])
                    p.op("vector", lambda E, bk=bk, cs=cs, ws=ws, c=cos_sb:
                         E.tensor_mul(kcw[:, ws], pb[bk][:], c[:, cs]))
                    kwv = p.op("vector", lambda E, bk=bk, cs=cs, ws=ws, s=sin_sb:
                               E.tensor_mul(kww[:, ws], pb[bk][:], s[:, cs]),
                               sem="vm")
                    p.bank_release(bk, "vm", kwv)
                p.wait("scalar", "vm", kwv)
                p.dma("scalar", rshw[0:64, :], kww[64:128, :], sem="sh")
                p.dma("scalar", rshw[64:128, :], kww[0:64, :], sem="sh")
                p.wait_all("vector", "sh")
                vfin = p.op("vector", lambda E:
                            E.tensor_add(rbw[:], kcw[:], rshw[:]), sem="vm")
                rope_free[0] = vfin
                p.wait("sync", "vm", vfin)
                p.dma("sync", dst_sc[ft, :, t0:t0 + 2048], rbw[:], sem="sp")
                sb2 = ft % 2
                p.wait("scalar", "vm", vfin)
                p.wait("scalar", "tm", norm_last[sb2])
                ssq = p.op("scalar", lambda E, sb2=sb2:
                           E.activation(sqw[sb2][:], rbw[:], AF.Square), sem="sm")
                spill_sq[1] = ssq
                pend_norm.append((ft, sb2, ssq))
                if len(pend_norm) > 1 or ft == H - 1:
                    todo = pend_norm[:-1] if ft < H - 1 else pend_norm
                    del pend_norm[:len(todo)]
                    for (ftn, sbn, ssqn) in todo:
                        p.wait("tensor", "sm", ssqn)
                        for j in range(4):
                            nb = 4 + j % 2
                            p.bank_acquire(nb)
                            tnv = mm(pb[nb][0:1, 0:512], ones_b[:],
                                     sqw[sbn][:, j * 512:(j + 1) * 512],
                                     True, True, sem="tm")
                            p.wait("vector", "tm", tnv)
                            rmx = p.op("vector", lambda E, nb=nb:
                                       E.reduce_max(stat[0][0:1, 0:1],
                                                    pb[nb][0:1, 0:512],
                                                    axis=AX.X), sem="vm")
                            p.wait("vector", "vm", rmx)
                            nv = p.op("vector", lambda E, ftn=ftn, no=nrm_off:
                                      E.tensor_tensor(
                                          nrm[0:1, no + ftn:no + ftn + 1],
                                          nrm[0:1, no + ftn:no + ftn + 1],
                                          stat[0][0:1, 0:1], ALU.max),
                                      sem="vm")
                            p.bank_release(nb, "vm", nv)
                        norm_last[sbn] = tnv
                wbuf_free[wb] = last_mm_w

        def proj_v(t0):
            p.wait("tensor", "vm", p.cnt["vm"])
            rb_free = [0, 0, 0, 0]
            for fc in range(4):
                p.wait("gpsimd", "tm", p.cnt["tm"])
                p.wait("gpsimd", "vm", p.cnt["vm"])
                p.wait("gpsimd", "sm", p.cnt["sm"])
                p.dma("gpsimd",
                      wv_sb.ap().rearrange("p (et f) -> p et f", f=512),
                      bass.AP(wvT, fc * 512,
                              [[E, 128], [128 * E, ET], [1, 512]]),
                      sem="wg0" if fc % 2 == 0 else "wg1")
                p.wait_all("tensor", "wg0" if fc % 2 == 0 else "wg1")
                for tt in range(16):
                    bk = 2 + tt % 2
                    p.bank_acquire(bk)
                    for et in range(ET):
                        v = mm(pb[bk][:, :], xnt[et][:, tt * 128:(tt + 1) * 128],
                               wv_sb[:, et * 512:(et + 1) * 512],
                               et == 0, et == ET - 1,
                               sem="tm" if et == ET - 1 else None)
                    rb = tt % 4
                    p.wait("scalar", "tm", v)
                    p.wait_all("scalar", "sp")
                    sv = p.op("scalar", lambda E, bk=bk, rb=rb:
                              E.copy(vst[rb][:], pb[bk][:]), sem="sm")
                    p.bank_release(bk, "sm", sv)
                    p.wait("sync", "sm", sv)
                    kt_g = t0 // 128 + tt
                    for j in range(4):
                        p.dma("sync", v_sc[fc * 4 + j, :, kt_g, :],
                              vst[rb][:, j * 128:(j + 1) * 128], sem="sp")

        if "noln" not in phases:
            ln_group(xo, 0)
            p.wait("sync", "vm", p.cnt["vm"])
            for et in range(ET):
                xnt_spill[et] = p.dma("sync", xnt_sc[et, :, :], xnt[et][:],
                                      sem="sp")
        if "noq" not in phases:
            proj_kq(q_sc, wqT, c_cosq, c_sinq, 0, "q")
        if "nokv" not in phases:
            for g in range(2):
                ln_group(xb, g * 2048)
                proj_kq(k_sc, wkT, c_cosk, c_sink, g * 2048, "k")
                proj_v(g * 2048)

        p.wait("vector", "sm", p.cnt["sm"])
        p.wait("vector", "vm", p.cnt["vm"])
        vv0 = p.op("vector", lambda E: E.tensor_tensor(
            nbz[0:1, :], nrm[0:1, 0:H], nrm[0:1, H:2 * H], ALU.mult), sem="vm")
        p.wait("scalar", "vm", vv0)
        sn = p.op("scalar", lambda E:
                  E.activation(nbz[0:1, :], nbz[0:1, :], AF.Sqrt), sem="sm")
        p.wait("vector", "sm", sn)
        vn = p.op("vector", lambda E: E.tensor_scalar(
            nbz[0:1, :], nbz[0:1, :], -1.02 * INV_SQRT_D, None, ALU.mult),
            sem="vm")
        p.wait("tensor", "vm", vn)
        p.bank_acquire(6)
        tn = mm(pb[6][:, 0:H], ones2[:], nbz[:], True, True, sem="tm")
        p.wait("vector", "tm", tn)
        nd = p.op("vector", lambda E: E.tensor_copy(negB[:], pb[6][:, 0:H]),
                  sem="vm")
        p.bank_release(6, "vm", nd)

    off[0] = 0
    oT = [A(f"oT{i}", [128, TOK], BF16) for i in range(H)]
    kbuf = [A(f"kbuf{i}", [128, T], BF16) for i in range(2)]
    vbuf = [A(f"vbuf{i}", [128, 32, 132], BF16) for i in range(2)]
    qbuf = [A(f"qbuf{i}", [128, TOK], BF16) for i in range(2)]
    ptb = [A(f"ptb{i}", [128, 512], BF16) for i in range(4)]
    mk_sb = A("mk_sb", [128, NSLOT * 8 * 512], U8)
    ost = [A(f"ost{i}", [128, 128], BF16) for i in range(2)]
    rcp = A("rcp", [128, 4], F32)
    assert off[0] <= AR

    if "p2" in phases:
        p.barrier()
        p.dma("sync",
              mk_sb.ap().rearrange("p (si mi q) -> p si mi q",
                                   si=NSLOT, mi=8),
              masks.ap(), sem="cs")
        p.wait_all("vector", "cs")
        p.op("vector", lambda E: E.memset(vbuf[0].ap()[:, :, 128:129], 1.0))
        mset = p.op("vector", lambda E: E.memset(vbuf[1].ap()[:, :, 128:129], 1.0),
                    sem="vm")
        p.wait("sync", "vm", mset)

        hl_free = [0, 0]
        ost_free = [0, 0]
        for h in range(H):
            hb = h % 2
            hk = "hk0" if hb == 0 else "hk1"
            p.wait("sync", "tm", hl_free[hb])
            p.dma("sync", kbuf[hb][:], k_sc[h, :, :], sem=hk)
            p.dma("sync", vbuf[hb].ap()[:, :, 0:128], v_sc[h, :, :, :], sem=hk)
            p.dma("sync", qbuf[hb][:], q_sc[h, :, :], sem=hk)
            p.wait_all("tensor", hk)
            last_tm = 0
            pt_free = [0, 0, 0, 0]
            for si in range(NSLOT):
                nkt = 4 * EXT[si]
                qs = si * 512
                for qq in range(4):
                    p.bank_acquire(2 + qq)

                SBANKS = (0, 1, 7)

                def s_mm(kt):
                    sbk = SBANKS[kt % 3]
                    p.bank_acquire(sbk)
                    return mm(pb[sbk][:, :],
                              kbuf[hb][:, kt * 128:(kt + 1) * 128],
                              qbuf[hb][:, qs:qs + 512], True, True, sem="tm")

                tvs = [s_mm(0)]
                if nkt > 1:
                    tvs.append(s_mm(1))
                for kt in range(nkt):
                    if kt + 2 < nkt:
                        tvs.append(s_mm(kt + 2))
                    sbk = SBANKS[kt % 3]
                    slot = kt % 4
                    p.wait("scalar", "tm", pt_free[slot])
                    p.wait("scalar", "tm", tvs[kt])
                    se = p.op("scalar", lambda E, sbk=sbk, slot=slot, h=h:
                              E.activation(ptb[slot][:], pb[sbk][:], AF.Exp,
                                           bias=negB[:, h:h + 1],
                                           scale=INV_SQRT_D), sem="sm")
                    p.bank_release(sbk, "sm", se)
                    if kt >= nkt - 8:
                        mi = kt - (nkt - 8)
                        mo = (si * 8 + mi) * 512
                        p.wait("vector", "sm", se)
                        sv = p.op("vector", lambda E, slot=slot, mo=mo:
                                  E.tensor_mul(ptb[slot][:], ptb[slot][:],
                                               mk_sb[:, mo:mo + 512]), sem="vm")
                        p.wait("tensor", "vm", sv)
                    else:
                        p.wait("tensor", "sm", se)
                    for qq in range(4):
                        v = mm(pb[2 + qq][:, 0:129],
                               ptb[slot][:, qq * 128:(qq + 1) * 128],
                               vbuf[hb].ap()[:, kt, 0:129],
                               kt == 0, kt == nkt - 1,
                               sem="tm" if qq == 3 else None)
                    pt_free[slot] = v
                    last_tm = v
                p.wait("vector", "tm", last_tm)
                for qq in range(4):
                    ob = qq % 2
                    rv = p.op("vector", lambda E, qq=qq:
                              E.reciprocal(rcp[:, qq:qq + 1],
                                           pb[2 + qq][:, 128:129]), sem="vm")
                    p.wait("vector", "vm", rv)
                    p.wait("vector", "tm", ost_free[ob])
                    vo = p.op("vector", lambda E, qq=qq, ob=ob: E.tensor_scalar(
                        ost[ob][:], pb[2 + qq][:, 0:128], rcp[:, qq:qq + 1],
                        None, ALU.mult), sem="vm")
                    p.bank_release(2 + qq, "vm", vo)
                    p.wait("tensor", "vm", vo)
                    p.bank_acquire(6)
                    tt_ = transpose(pbb[6][:, 0:128], ost[ob][:], sem="tm")
                    ost_free[ob] = tt_
                    p.wait("scalar", "tm", tt_)
                    so = p.op("scalar", lambda E, h=h, qs=qs, qq=qq:
                              E.copy(oT[h][:, qs + qq * 128: qs + (qq + 1) * 128],
                                     pbb[6][:, 0:128]), sem="sm")
                    p.bank_release(6, "sm", so)
            hl_free[hb] = last_tm

    off[0] = 64 * 1024
    wo_sb = [A(f"wosb{i}", [128, ET * 512], BF16) for i in range(2)]
    xres = [A(f"xres{i}", [128, 512], F32) for i in range(2)]
    rst = [A(f"rst{i}", [128, 512], F32) for i in range(2)]
    assert off[0] <= AR

    if "po" in phases:
        p.barrier()
        wo_free = [0, 0]
        rs_free = [0, 0]
        xr_free = [0, 0]
        for fc in range(4):
            wb = fc % 2
            p.wait("gpsimd", "tm", wo_free[wb])
            p.dma("gpsimd",
                  wo_sb[wb].ap().rearrange("p (et f) -> p et f", f=512),
                  bass.AP(woT, fc * 512, [[E, 128], [128 * E, ET], [1, 512]]),
                  sem="wg0" if wb == 0 else "wg1")
            p.wait_all("tensor", "wg0" if wb == 0 else "wg1")
            last_mm_w = 0
            for tt in range(16):
                bk = tt % 2
                p.bank_acquire(bk)
                p.wait("sync", "vm", xr_free[bk])
                p.dma("sync", xres[bk][:],
                      xo[tt * 128:(tt + 1) * 128, fc * 512:(fc + 1) * 512],
                      sem="xl0" if bk == 0 else "xl1")
                for et in range(ET):
                    v = mm(pb[bk][:, :], oT[et][:, tt * 128:(tt + 1) * 128],
                           wo_sb[wb][:, et * 512:(et + 1) * 512],
                           et == 0, et == ET - 1,
                           sem="tm" if et == ET - 1 else None)
                last_mm_w = v
                rb = tt % 2
                p.wait_all("vector", "rg")
                p.wait("vector", "tm", v)
                p.wait_all("vector", "xl0" if bk == 0 else "xl1")
                vv = p.op("vector", lambda E, bk=bk, rb=rb:
                          E.tensor_add(rst[rb][:], pb[bk][:], xres[bk][:]),
                          sem="vm")
                p.bank_release(bk, "vm", vv)
                xr_free[bk] = vv
                p.wait("sync", "vm", vv)
                p.dma("sync",
                      r_sc[tt * 128:(tt + 1) * 128, fc * 512:(fc + 1) * 512],
                      rst[rb][:], sem="rg")
            wo_free[wb] = last_mm_w

    off[0] = 0
    xno = [A(f"xno{i}", [128, TOK], BF16) for i in range(ET)]
    gw_sb = [A(f"gwsb{i}", [128, E], BF16) for i in range(2)]
    uw_sb = [A(f"uwsb{i}", [128, E], BF16) for i in range(2)]
    g8 = [A(f"g8_{i}", [128, E], FP8) for i in range(2)]
    u8 = [A(f"u8_{i}", [128, E], FP8) for i in range(2)]
    sgt = A("sgt", [128, 512], F32)
    upt = A("upt", [128, 512], F32)
    fst = [A(f"fst{i}", [128, 512], BF16) for i in range(2)]
    assert off[0] <= AR

    if "p3" in phases:
        p.barrier()
        for et in range(ET):
            p.dma("sync", xno[et][:], xnt_sc[et, :, :], sem="cs")
        p.wait_all("tensor", "cs")
        gw_free = [0, 0]
        fs_free = [0, 0]
        sg_free = [0]
        up_free = [0]
        for it in range(IT):
            wb = it % 2
            wg = "wg0" if wb == 0 else "wg1"
            p.wait("gpsimd", "tm", gw_free[wb])
            p.dma("gpsimd", g8[wb][:],
                  bass.AP(gw, it * 128, [[I, 128], [128 * I, ET], [1, 128]]),
                  sem=wg)
            p.dma("gpsimd", u8[wb][:],
                  bass.AP(uw, it * 128, [[I, 128], [128 * I, ET], [1, 128]]),
                  sem=wg)
            p.wait_all("gpsimd", wg)
            p.dma("gpsimd", gw_sb[wb][:], g8[wb][:], sem=wg)
            p.dma("gpsimd", uw_sb[wb][:], u8[wb][:], sem=wg)
            p.wait_all("tensor", wg)
            last_mm_w = 0
            for tc in range(4):
                bkg = tc % 2
                bku = 2 + tc % 2
                p.bank_acquire(bkg)
                p.bank_acquire(bku)
                for et in range(ET):
                    mm(pb[bkg][:, :], gw_sb[wb][:, et * 128:(et + 1) * 128],
                       xno[et][:, tc * 512:(tc + 1) * 512], et == 0, et == ET - 1)
                for et in range(ET):
                    v = mm(pb[bku][:, :], uw_sb[wb][:, et * 128:(et + 1) * 128],
                           xno[et][:, tc * 512:(tc + 1) * 512],
                           et == 0, et == ET - 1,
                           sem="tm" if et == ET - 1 else None)
                last_mm_w = v
                p.wait("scalar", "tm", v)
                p.wait("scalar", "vm", sg_free[0])
                sg1 = p.op("scalar", lambda E, bkg=bkg, it=it: E.activation(
                    sgt[:], pb[bkg][:], AF.Silu, scale=gs_sb[:, it:it + 1]),
                    sem="sm")
                p.bank_release(bkg, "sm", sg1)
                fb = tc % 2
                p.wait("vector", "sm", sg1)
                p.wait("vector", "vm", up_free[0])
                uv = p.op("vector", lambda E, bku=bku, it=it: E.tensor_scalar(
                    upt[:], pb[bku][:], us_sb[:, it:it + 1], None, ALU.mult),
                    sem="vm")
                p.wait("vector", "vm", uv)
                p.wait_all("vector", "sp")
                vf = p.op("vector", lambda E, fb=fb:
                          E.tensor_mul(fst[fb][:], sgt[:], upt[:]), sem="vm")
                sg_free[0] = vf
                up_free[0] = vf
                p.bank_release(bku, "vm", vf)
                p.wait("sync", "vm", vf)
                for j in range(4):
                    p.dma("sync",
                          fuse_sc[tc * 4 + j, :, it * 128:(it + 1) * 128],
                          fst[fb][:, j * 128:(j + 1) * 128], sem="sp")
            gw_free[wb] = last_mm_w

        dwq = [[nc.alloc_sbuf_tensor_at(f"dwq0_{i}", [128, 512], BF16,
                                        offset=abase + 94 * 1024 + i * 1024)
                for i in range(IT)], None]
        d8b = [nc.alloc_sbuf_tensor_at(f"d8b{i}", [128, 512], FP8,
                                       offset=abase + 143 * 1024 + i * 512)
               for i in range(4)]
        off[0] = 0
        dwq[1] = [A(f"dwq1_{i}", [128, 512], BF16) for i in range(IT)]
        ftb = [A(f"ftb{i}", [128, IT * 128], BF16) for i in range(2)]
        rres = [A(f"rres{i}", [128, 512], F32) for i in range(2)]
        ytmp = [A(f"ytmp{i}", [128, 512], F32) for i in range(2)]
        yst = [A(f"yst{i}", [128, 512], F32) for i in range(2)]
        assert off[0] <= 94 * 1024

        dwq_free = [0, 0]

        def load_quarter(q4):
            par = q4 % 2
            wg = "wg0" if par == 0 else "wg1"
            p.wait("gpsimd", "tm", dwq_free[par])
            for g in range(IT // 4):
                p.wait_all("gpsimd", wg)
                for u in range(4):
                    it = g * 4 + u
                    p.dma("gpsimd", d8b[u][:],
                          dw[it * 128:(it + 1) * 128,
                             q4 * 512:(q4 + 1) * 512], sem="d8s")
                p.wait_all("gpsimd", "d8s")
                for u in range(4):
                    it = g * 4 + u
                    p.dma("gpsimd", dwq[par][it][:], d8b[u][:], sem=wg)

        load_quarter(0)
        p.barrier()
        ft_free = [0, 0]
        rr_free = [0, 0]
        y_pending = None
        for q4 in range(4):
            par = q4 % 2
            if q4 + 1 < 4:
                load_quarter(q4 + 1)
            p.wait_all("tensor", "wg0" if par == 0 else "wg1")
            last_mm = 0
            for tt in range(16):
                fb = tt % 2
                fbs = "fb0" if fb == 0 else "fb1"
                rp = tt % 2
                drs = "dr0" if rp == 0 else "dr1"
                p.wait("sync", "tm", ft_free[fb])
                p.dma("sync", ftb[fb][:], fuse_sc[tt, :, :], sem=fbs)
                p.wait("sync", "vm", rr_free[rp])
                p.dma("sync", rres[rp][:],
                      r_sc[tt * 128:(tt + 1) * 128, q4 * 512:(q4 + 1) * 512],
                      sem=drs)
                if y_pending is not None:
                    pv, ptt, pcs, ppar = y_pending
                    p.wait("sync", "vm", pv)
                    p.dma("sync", y[ptt * 128:(ptt + 1) * 128, pcs],
                          yst[ppar][:], sem="ou0" if ppar == 0 else "ou1")
                    y_pending = None
                p.wait_all("tensor", fbs)
                bk = tt % 4
                p.bank_acquire(bk)
                for it in range(IT):
                    v = mm(pb[bk][:, :], ftb[fb][:, it * 128:(it + 1) * 128],
                           dwq[par][it][:], it == 0, it == IT - 1,
                           sem="tm" if it == IT - 1 else None)
                last_mm = v
                ous = "ou0" if par == 0 else "ou1"
                p.wait_all("vector", ous)
                p.wait("vector", "tm", v)
                p.wait_all("vector", drs)
                cs = slice(q4 * 512, (q4 + 1) * 512)
                mv = p.op("vector", lambda E, bk=bk, cs=cs, par=par:
                          E.tensor_mul(ytmp[par][:], pb[bk][:], dsb[:, cs]),
                          sem="vm")
                p.wait("vector", "vm", mv)
                vv = p.op("vector", lambda E, par=par, rp=rp:
                          E.tensor_add(yst[par][:], ytmp[par][:], rres[rp][:]),
                          sem="vm")
                p.bank_release(bk, "vm", vv)
                rr_free[rp] = vv
                y_pending = (vv, tt, cs, par)
                ft_free[fb] = last_mm
            dwq_free[par] = last_mm
        pv, ptt, pcs, ppar = y_pending
        p.wait("sync", "vm", pv)
        p.dma("sync", y[ptt * 128:(ptt + 1) * 128, pcs], yst[ppar][:],
              sem="ou0" if ppar == 0 else "ou1")

    if "p3" not in phases:
        zout = SB("zout", [128, 512], F32)
        zv = p.op("vector", lambda E: E.memset(zout[:], 0.0), sem="vm")
        p.wait("sync", "vm", zv)
        for tt in range(16):
            for j in range(4):
                p.dma("sync", y[tt * 128:(tt + 1) * 128, j * 512:(j + 1) * 512],
                      zout[:], sem="out")

    p.wait("sync", "out", p.cnt["out"])
    p.wait("sync", "ou0", p.cnt["ou0"])
    p.wait("sync", "ou1", p.cnt["ou1"])

    sem_handles = {s: es.enter_context(nc.semaphore(s))
                   for s in p.cnt}
    p.emit(nc, sem_handles)
    nc._keepalive = es
    nc._dbg = {"oT": [t.name for t in oT], "xnt": [t.name for t in xnt],
               "xno": [t.name for t in xno]}
    return nc


def host_inputs(half):
    cos, sin = rope_tables()
    cosT = np.ascontiguousarray(cos.T)
    sinT = np.ascontiguousarray(sin.T).copy()
    sinT[64:, :] *= -1.0
    rows = own_rows(half)
    mask = np.zeros((NSLOT, 8, 128, 512), np.uint8)
    for si in range(NSLOT):
        nkt = 4 * EXT[si]
        gq = rows[si * 512:(si + 1) * 512]
        for mi in range(8):
            kt = nkt - 8 + mi
            kpos = kt * 128 + np.arange(128)
            mask[si, mi] = (kpos[:, None] <= gq[None, :]).astype(np.uint8)
    return {
        "rows": rows,
        "cosq": np.ascontiguousarray(cosT[:, rows]),
        "sinq": np.ascontiguousarray(sinT[:, rows]),
        "cosk": cosT,
        "sink": np.ascontiguousarray(sinT),
        "masks": np.ascontiguousarray(mask.transpose(2, 0, 1, 3)),
    }


def make_in_map(x_full, half, b, wqT, wkT, wvT, woT, gw, uw, dw, gsv, usv, dsv,
                identb):
    hd = host_inputs(half)
    x_b = np.ascontiguousarray(x_full[b])
    return hd["rows"], {
        "xb": x_b,
        "xo": np.ascontiguousarray(x_b[hd["rows"]]),
        "wqT": wqT, "wkT": wkT, "wvT": wvT, "woT": woT,
        "gw": gw, "uw": uw, "dw": dw,
        "gs": gsv, "us": usv, "ds": dsv,
        "cosk": hd["cosk"], "sink": hd["sink"],
        "cosq": hd["cosq"], "sinq": hd["sinq"],
        "identb": identb, "masks": hd["masks"],
    }


def kernel(L0_attn_x, L0_attn_wq, L0_attn_wk, L0_attn_wv, L0_attn_wo,
           L0_ffn_gate_w, L0_ffn_up_w, L0_ffn_down_w,
           L0_ffn_gate_scale, L0_ffn_up_scale, L0_ffn_down_scale):
    nc = build_nc()
    wqT = np.ascontiguousarray(L0_attn_wq.T)
    wkT = np.ascontiguousarray(L0_attn_wk.T)
    wvT = np.ascontiguousarray(L0_attn_wv.T)
    woT = np.ascontiguousarray(L0_attn_wo.T)
    identb = np.eye(128, dtype=ml_dtypes.bfloat16)
    in_maps, rows = [], {}
    for c in range(NCORE):
        b, half = c // 2, c % 2
        r, im = make_in_map(L0_attn_x, half, b, wqT, wkT, wvT, woT,
                            L0_ffn_gate_w, L0_ffn_up_w, L0_ffn_down_w,
                            L0_ffn_gate_scale, L0_ffn_up_scale,
                            L0_ffn_down_scale, identb)
        rows[half] = r
        in_maps.append(im)
    res = run_bass_kernel_spmd(nc, in_maps, core_ids=list(range(NCORE)))
    out = np.empty((B, T, E), np.float32)
    for c in range(NCORE):
        b, half = c // 2, c % 2
        out[b, rows[half]] = res.results[c]["y"]
    return out


# version 11
# speedup vs baseline: 1.0275x; 1.0275x over previous
import numpy as np
import ml_dtypes
from contextlib import ExitStack

import concourse.bass as bass
import concourse.mybir as mybir
from concourse.bass_utils import run_bass_kernel_spmd

F32 = mybir.dt.float32
BF16 = mybir.dt.bfloat16
FP8 = mybir.dt.float8e4
U8 = mybir.dt.uint8
AF = mybir.ActivationFunctionType
ALU = mybir.AluOpType
AX = mybir.AxisListType

B, T, E = 4, 4096, 2048
H, Dh = 16, 128
I = 6144
NCORE = 8
TOK = 2048
NSLOT = 4
EXT = [8, 6, 4, 2]
CHUNKS = {0: [7, 5, 2, 0], 1: [6, 4, 3, 1]}
ET = E // 128
IT = I // 128
INV_SQRT_D = float(np.float32(1.0) / np.float32(np.sqrt(np.float32(Dh))))
ENGINES = ("sync", "gpsimd", "tensor", "vector", "scalar")
COMPUTE = ("tensor", "vector", "scalar")
DTS = {F32: 4, BF16: 2, U8: 1, FP8: 1}


def own_rows(half):
    return np.concatenate([np.arange(c * 512, (c + 1) * 512) for c in CHUNKS[half]])


_TABLES = None


def rope_tables():
    global _TABLES
    if _TABLES is None:
        f32 = np.float32
        e = np.arange(0, Dh, 2, dtype=f32) / f32(Dh)
        inv = (f32(1.0) / np.power(f32(1e6), e)).astype(f32)
        f = np.arange(T, dtype=f32)[:, None] * inv[None, :]
        cos = np.concatenate([np.cos(f), np.cos(f)], -1).astype(f32)
        sin = np.concatenate([np.sin(f), np.sin(f)], -1).astype(f32)
        _TABLES = (cos, sin)
    return _TABLES


class Prog:

    def __init__(self):
        self.ops = {e: [] for e in ENGINES}
        self.cnt = {k: 0 for k in ("ds", "dg", "tm", "vm", "sm", "out", "cs",
                                    "xl0", "xl1", "wg0", "wg1", "sh", "sp",
                                    "hk0", "hk1", "dr0", "dr1", "fb0", "fb1",
                                    "ou0", "ou1", "rg", "d8s")}
        self.bank = {}

    def wait(self, eng, sem, val):
        if val:
            self.ops[eng].append(lambda E, s=sem, v=val: E.wait_ge(self.sem[s], v))

    def wait_all(self, eng, sem):
        self.wait(eng, sem, self.cnt[sem])

    def dma(self, eng, out, in_, sem=None):
        if sem is None:
            sem = "ds" if eng == "sync" else "dg"
        self.cnt[sem] += 16
        v = self.cnt[sem]

        def f(E, out=out, in_=in_, s=sem):
            E.dma_start(out, in_).then_inc(self.sem[s], 16)

        self.ops[eng].append(f)
        return v

    def op(self, eng, fn, sem=None):
        v = None
        if sem is not None:
            self.cnt[sem] += 1
            v = self.cnt[sem]

        def f(E, fn=fn, s=sem):
            inst = fn(E)
            if s is not None:
                inst.then_inc(self.sem[s], 1)

        self.ops[eng].append(f)
        return v

    def bank_acquire(self, bk):
        g = self.bank.get(bk)
        if g:
            self.wait("tensor", g[0], g[1])
            self.bank[bk] = None

    def bank_release(self, bk, sem, val):
        self.bank[bk] = (sem, val)

    def barrier(self, engines=ENGINES):
        for e in engines:
            for s, v in list(self.cnt.items()):
                self.wait(e, s, v)

    def emit(self, nc, sem_handles):
        self.sem = sem_handles
        with nc.Block() as block:
            @block.sync
            def _(e):
                for f in self.ops["sync"]:
                    f(e)

            @block.gpsimd
            def _(e):
                for f in self.ops["gpsimd"]:
                    f(e)

            @block.tensor
            def _(e):
                for f in self.ops["tensor"]:
                    f(e)

            @block.vector
            def _(e):
                for f in self.ops["vector"]:
                    f(e)

            @block.scalar
            def _(e):
                for f in self.ops["scalar"]:
                    f(e)


def build_nc(phases=("p1", "p2", "po", "p3"), dbg=False):
    nc = bass.Bass("TRN2", target_bir_lowering=False,
                   detect_race_conditions=False)
    SCR = "ExternalOutput" if dbg else "Internal"

    xb = nc.dram_tensor("xb", [T, E], F32, kind="ExternalInput")
    xo = nc.dram_tensor("xo", [TOK, E], F32, kind="ExternalInput")
    wqT = nc.dram_tensor("wqT", [E, E], F32, kind="ExternalInput")
    wkT = nc.dram_tensor("wkT", [E, E], F32, kind="ExternalInput")
    wvT = nc.dram_tensor("wvT", [E, E], F32, kind="ExternalInput")
    woT = nc.dram_tensor("woT", [E, E], F32, kind="ExternalInput")
    gw = nc.dram_tensor("gw", [E, I], F32, kind="ExternalInput")
    uw = nc.dram_tensor("uw", [E, I], F32, kind="ExternalInput")
    dw = nc.dram_tensor("dw", [I, E], F32, kind="ExternalInput")
    gs = nc.dram_tensor("gs", [I], F32, kind="ExternalInput")
    us = nc.dram_tensor("us", [I], F32, kind="ExternalInput")
    ds_ = nc.dram_tensor("ds", [E], F32, kind="ExternalInput")
    cosk = nc.dram_tensor("cosk", [128, T], F32, kind="ExternalInput")
    sink = nc.dram_tensor("sink", [128, T], F32, kind="ExternalInput")
    cosq = nc.dram_tensor("cosq", [128, TOK], F32, kind="ExternalInput")
    sinq = nc.dram_tensor("sinq", [128, TOK], F32, kind="ExternalInput")
    identb = nc.dram_tensor("identb", [128, 128], BF16, kind="ExternalInput")
    masks = nc.dram_tensor("masks", [128, NSLOT, 8, 512], U8, kind="ExternalInput")

    k_sc = nc.dram_tensor("k_sc", [H, 128, T], BF16, kind=SCR)
    v_sc = nc.dram_tensor("v_sc", [H, 128, T // 128, 128], BF16, kind=SCR)
    q_sc = nc.dram_tensor("q_sc", [H, 128, TOK], BF16, kind=SCR)
    xnt_sc = nc.dram_tensor("xnt_sc", [ET, 128, TOK], BF16, kind=SCR)
    fuse_sc = nc.dram_tensor("fuse_sc", [TOK // 128, 128, IT * 128], BF16, kind=SCR)
    r_sc = nc.dram_tensor("r_sc", [TOK, E], F32, kind=SCR)
    y = nc.dram_tensor("y", [TOK, E], F32, kind="ExternalOutput")

    es = ExitStack()
    SB = lambda nm, sh, dt: es.enter_context(nc.sbuf_tensor(nm, sh, dt))
    const_idb = SB("c_idb", [128, 128], BF16)
    negB = SB("negB", [128, H], F32)
    gs_sb = SB("gs_sb", [128, IT], F32)
    us_sb = SB("us_sb", [128, IT], F32)
    dsb = SB("dsb", [128, E], F32)
    ones_b = SB("ones_b", [128, 1], BF16)
    ones2 = SB("ones2", [128, 128], F32)
    nrm = SB("nrm", [1, 2 * H], F32)
    nbz = SB("nbz", [128, H], F32)

    pb = [es.enter_context(nc.psum_tensor(f"pb{i}", [128, 512], F32))
          for i in range(8)]
    pbb = [t.bitcast(BF16) for t in pb]

    AR = 189 * 1024
    arena = nc.alloc_sbuf_tensor("arena", [128, AR], mybir.dt.uint8)
    abase = nc.lookup_mloc(arena).addr
    off = [0]

    def A(nm, sh, dt):
        nbytes = int(np.prod(sh[1:])) * DTS[dt]
        o = (off[0] + 31) // 32 * 32
        t = nc.alloc_sbuf_tensor_at(nm, sh, dt, offset=abase + o)
        off[0] = o + nbytes
        assert off[0] <= AR, f"arena overflow at {nm}: {off[0]}"
        return t

    p = Prog()

    def mm(psum_ap, lhsT, rhs, start, stop, sem=None):
        return p.op("tensor",
                    lambda E, o=psum_ap, l=lhsT, r=rhs, a=start, b=stop:
                    E.matmul(o, l, r, start=a, stop=b), sem=sem)

    def transpose(psum_ap, in_ap, sem=None):
        return p.op("tensor",
                    lambda E, o=psum_ap, i=in_ap:
                    E.transpose(o, i, const_idb[:]), sem=sem)

    c_cosk = A("c_cosk", [128, T], F32)
    c_sink = A("c_sink", [128, T], F32)
    c_cosq = A("c_cosq", [128, TOK], F32)
    c_sinq = A("c_sinq", [128, TOK], F32)
    _off_tab = off[0]
    dz = A("dz", [128, E], F32)
    off[0] = _off_tab

    p.dma("sync", c_cosk[:], cosk[:], sem="cs")
    p.dma("sync", c_sink[:], sink[:], sem="cs")
    p.dma("sync", c_cosq[:], cosq[:], sem="cs")
    p.dma("sync", c_sinq[:], sinq[:], sem="cs")
    id_ld = p.dma("sync", const_idb[:], identb[:], sem="cs")
    for j in range(IT):
        p.dma("sync", gs_sb[:, j:j + 1], bass.AP(gs, j * 128, [[1, 128], [1, 1]]),
              sem="cs")
        p.dma("sync", us_sb[:, j:j + 1], bass.AP(us, j * 128, [[1, 128], [1, 1]]),
              sem="cs")
    p.op("vector", lambda E: E.memset(ones_b[:], 1.0))
    p.op("vector", lambda E: E.memset(ones2[:], 1.0))
    p.op("vector", lambda E: E.memset(nbz[:], 0.0))
    zd = p.op("vector", lambda E: E.memset(dz[:], 0.0), sem="vm")
    p.op("vector", lambda E: E.memset(nrm[:], 0.0), sem="vm")
    p.wait("sync", "vm", zd)
    p.dma("sync", dz[0:1, :], bass.AP(ds_, 0, [[E, 1], [1, E]]), sem="cs")
    p.wait_all("tensor", "cs")
    p.wait("tensor", "vm", zd)
    for j in range(4):
        p.bank_acquire(7)
        tv = mm(pb[7][:, :], ones2[:], dz[:, j * 512:(j + 1) * 512],
                True, True, sem="tm")
        p.wait("vector", "tm", tv)
        ev = p.op("vector",
                  lambda E, j=j: E.tensor_copy(dsb[:, j * 512:(j + 1) * 512],
                                               pb[7][:, :]),
                  sem="vm")
        p.bank_release(7, "vm", ev)

    xnt = [A(f"xnt{i}", [128, 2048], BF16) for i in range(ET)]
    x_sb = [A(f"xsb{i}", [128, E], F32) for i in range(2)]
    sqb = A("sqb", [128, E], F32)
    xnb = [A(f"xnb{i}", [128, E], BF16) for i in range(2)]
    w_sb = [A(f"wsb{i}", [128, E], BF16) for i in range(2)]
    _u0 = (off[0] + 31) // 32 * 32
    kcw = A("kcw", [128, 2048], F32)
    kww = A("kww", [128, 2048], F32)
    rshw = A("rshw", [128, 2048], F32)
    rbw = A("rbw", [128, 2048], BF16)
    sqw = [A(f"sqw{i}", [128, 2048], BF16) for i in range(2)]
    wv_sb = nc.alloc_sbuf_tensor_at("wvsb", [128, ET * 512], BF16,
                                    offset=abase + _u0)
    vst = [nc.alloc_sbuf_tensor_at(f"vst{i}", [128, 512], BF16,
                                   offset=abase + _u0 + 16 * 1024 + i * 1024)
           for i in range(4)]
    stat = [A(f"stat{i}", [128, 4], F32) for i in range(2)]
    assert off[0] <= AR
    E_ = float(E)

    xnt_spill = [0] * ET

    if "p1" in phases:
        ln_x_free = [0, 0]
        xnb_free = [0, 0]
        sq_free = [0]

        def ln_group(src, t0):
            p.wait_all("tensor", "cs")
            for tb in range(16):
                buf = tb % 2
                p.wait("sync", "sm", ln_x_free[buf])
                p.dma("sync", x_sb[buf][:],
                      src[t0 + tb * 128: t0 + (tb + 1) * 128, :],
                      sem="xl0" if buf == 0 else "xl1")
                p.wait_all("vector", "xl0" if buf == 0 else "xl1")
                r1 = p.op("vector", lambda E, b=buf:
                          E.reduce_sum(stat[b][:, 0:1], x_sb[b][:], axis=AX.X),
                          sem="vm")
                p.wait("vector", "vm", r1)
                r2 = p.op("vector", lambda E, b=buf: E.tensor_scalar(
                    stat[b][:, 1:2], stat[b][:, 0:1], 1.0 / E_, None, ALU.mult),
                    sem="vm")
                p.wait("vector", "vm", r2)
                vsub = p.op("vector", lambda E, b=buf: E.tensor_scalar(
                    x_sb[b][:], x_sb[b][:], stat[b][:, 1:2], None, ALU.subtract),
                    sem="vm")
                p.wait("scalar", "vm", vsub)
                p.wait("scalar", "vm", sq_free[0])
                ssq = p.op("scalar", lambda E, b=buf:
                           E.activation(sqb[:], x_sb[b][:], AF.Square), sem="sm")
                p.wait("vector", "sm", ssq)
                vq = p.op("vector", lambda E, b=buf:
                          E.reduce_sum(stat[b][:, 2:3], sqb[:], axis=AX.X), sem="vm")
                sq_free[0] = vq
                p.wait("vector", "vm", vq)
                vv_ = p.op("vector", lambda E, b=buf: E.tensor_scalar(
                    stat[b][:, 2:3], stat[b][:, 2:3], 1.0 / E_, 1e-5,
                    ALU.mult, ALU.add), sem="vm")
                p.wait("scalar", "vm", vv_)
                sq_ = p.op("scalar", lambda E, b=buf:
                           E.activation(stat[b][:, 3:4], stat[b][:, 2:3], AF.Sqrt),
                           sem="sm")
                p.wait("vector", "sm", sq_)
                vrec = p.op("vector", lambda E, b=buf:
                            E.reciprocal(stat[b][:, 3:4], stat[b][:, 3:4]), sem="vm")
                p.wait("scalar", "vm", vrec)
                p.wait("scalar", "tm", xnb_free[buf])
                sxn = p.op("scalar", lambda E, b=buf: E.activation(
                    xnb[b][:], x_sb[b][:], AF.Copy, scale=stat[b][:, 3:4]), sem="sm")
                ln_x_free[buf] = sxn
                p.wait("tensor", "sm", sxn)
                for et in range(ET):
                    bk = et % 2
                    p.bank_acquire(bk)
                    tv = transpose(pbb[bk][:, 0:128],
                                   xnb[buf][:, et * 128:(et + 1) * 128], sem="tm")
                    if et >= ET - 2:
                        xnb_free[buf] = tv
                    if et % 2:
                        p.wait("vector", "tm", tv)
                        ev = p.op("vector", lambda E, et=et, tb=tb, bk=bk:
                                  E.tensor_copy(
                                      xnt[et][:, tb * 128:(tb + 1) * 128],
                                      pbb[bk][:, 0:128]), sem="vm")
                        p.bank_release(bk, "vm", ev)
                    else:
                        p.wait("scalar", "tm", tv)
                        ev = p.op("scalar", lambda E, et=et, tb=tb, bk=bk:
                                  E.copy(xnt[et][:, tb * 128:(tb + 1) * 128],
                                         pbb[bk][:, 0:128]), sem="sm")
                        p.bank_release(bk, "sm", ev)

        def proj_kq(dst_sc, wT_dram, cos_sb, sin_sb, t0, name):
            p.wait("tensor", "vm", p.cnt["vm"])
            p.wait("tensor", "sm", p.cnt["sm"])
            p.wait("sync", "sp", p.cnt["sp"])
            nrm_off = 0 if name == "q" else H
            wbuf_free = [p.cnt["tm"], p.cnt["tm"]]
            rope_free = [0]
            spill_sq = [0, 0]
            norm_last = [0, 0]
            pend_norm = []
            for ft in range(H):
                wb = ft % 2
                p.wait("gpsimd", "tm", wbuf_free[wb])
                p.dma("gpsimd", w_sb[wb][:],
                      bass.AP(wT_dram, ft * 128,
                              [[E, 128], [128 * E, ET], [1, 128]]),
                      sem="wg0" if wb == 0 else "wg1")
                p.wait_all("tensor", "wg0" if wb == 0 else "wg1")
                last_mm_w = 0
                kwv = 0
                for tc in range(4):
                    bk = tc
                    p.bank_acquire(bk)
                    for et in range(ET):
                        v = mm(pb[bk][:, :], w_sb[wb][:, et * 128:(et + 1) * 128],
                               xnt[et][:, tc * 512:(tc + 1) * 512],
                               et == 0, et == ET - 1,
                               sem="tm" if et == ET - 1 else None)
                    last_mm_w = v
                    cs = slice(t0 + tc * 512, t0 + (tc + 1) * 512)
                    ws = slice(tc * 512, (tc + 1) * 512)
                    p.wait("vector", "tm", v)
                    if tc == 0:
                        p.wait("vector", "vm", rope_free[0])
                        p.wait_all("vector", "sp")
                        p.wait("vector", "sm", spill_sq[1])
                    p.op("vector", lambda E, bk=bk, cs=cs, ws=ws, c=cos_sb:
                         E.tensor_mul(kcw[:, ws], pb[bk][:], c[:, cs]))
                    kwv = p.op("vector", lambda E, bk=bk, cs=cs, ws=ws, s=sin_sb:
                               E.tensor_mul(kww[:, ws], pb[bk][:], s[:, cs]),
                               sem="vm")
                    p.bank_release(bk, "vm", kwv)
                p.wait("scalar", "vm", kwv)
                p.dma("scalar", rshw[0:64, :], kww[64:128, :], sem="sh")
                p.dma("scalar", rshw[64:128, :], kww[0:64, :], sem="sh")
                p.wait_all("vector", "sh")
                vfin = p.op("vector", lambda E:
                            E.tensor_add(rbw[:], kcw[:], rshw[:]), sem="vm")
                rope_free[0] = vfin
                p.wait("sync", "vm", vfin)
                p.dma("sync", dst_sc[ft, :, t0:t0 + 2048], rbw[:], sem="sp")
                sb2 = ft % 2
                p.wait("scalar", "vm", vfin)
                p.wait("scalar", "tm", norm_last[sb2])
                ssq = p.op("scalar", lambda E, sb2=sb2:
                           E.activation(sqw[sb2][:], rbw[:], AF.Square), sem="sm")
                spill_sq[1] = ssq
                pend_norm.append((ft, sb2, ssq))
                if len(pend_norm) > 1 or ft == H - 1:
                    todo = pend_norm[:-1] if ft < H - 1 else pend_norm
                    del pend_norm[:len(todo)]
                    for (ftn, sbn, ssqn) in todo:
                        p.wait("tensor", "sm", ssqn)
                        for j in range(4):
                            nb = 4 + j % 2
                            p.bank_acquire(nb)
                            tnv = mm(pb[nb][0:1, 0:512], ones_b[:],
                                     sqw[sbn][:, j * 512:(j + 1) * 512],
                                     True, True, sem="tm")
                            p.wait("vector", "tm", tnv)
                            rmx = p.op("vector", lambda E, nb=nb:
                                       E.reduce_max(stat[0][0:1, 0:1],
                                                    pb[nb][0:1, 0:512],
                                                    axis=AX.X), sem="vm")
                            p.wait("vector", "vm", rmx)
                            nv = p.op("vector", lambda E, ftn=ftn, no=nrm_off:
                                      E.tensor_tensor(
                                          nrm[0:1, no + ftn:no + ftn + 1],
                                          nrm[0:1, no + ftn:no + ftn + 1],
                                          stat[0][0:1, 0:1], ALU.max),
                                      sem="vm")
                            p.bank_release(nb, "vm", nv)
                        norm_last[sbn] = tnv
                wbuf_free[wb] = last_mm_w

        def proj_v(t0):
            p.wait("tensor", "vm", p.cnt["vm"])
            p.wait("tensor", "sm", p.cnt["sm"])
            rb_free = [0, 0, 0, 0]
            for fc in range(4):
                p.wait("gpsimd", "tm", p.cnt["tm"])
                p.wait("gpsimd", "vm", p.cnt["vm"])
                p.wait("gpsimd", "sm", p.cnt["sm"])
                p.dma("gpsimd",
                      wv_sb.ap().rearrange("p (et f) -> p et f", f=512),
                      bass.AP(wvT, fc * 512,
                              [[E, 128], [128 * E, ET], [1, 512]]),
                      sem="wg0" if fc % 2 == 0 else "wg1")
                p.wait_all("tensor", "wg0" if fc % 2 == 0 else "wg1")
                for tt in range(16):
                    bk = 2 + tt % 2
                    p.bank_acquire(bk)
                    for et in range(ET):
                        v = mm(pb[bk][:, :], xnt[et][:, tt * 128:(tt + 1) * 128],
                               wv_sb[:, et * 512:(et + 1) * 512],
                               et == 0, et == ET - 1,
                               sem="tm" if et == ET - 1 else None)
                    rb = tt % 4
                    p.wait("scalar", "tm", v)
                    p.wait_all("scalar", "sp")
                    sv = p.op("scalar", lambda E, bk=bk, rb=rb:
                              E.copy(vst[rb][:], pb[bk][:]), sem="sm")
                    p.bank_release(bk, "sm", sv)
                    p.wait("sync", "sm", sv)
                    kt_g = t0 // 128 + tt
                    for j in range(4):
                        p.dma("sync", v_sc[fc * 4 + j, :, kt_g, :],
                              vst[rb][:, j * 128:(j + 1) * 128], sem="sp")

        if "noln" not in phases:
            ln_group(xo, 0)
            p.wait("sync", "vm", p.cnt["vm"])
            for et in range(ET):
                xnt_spill[et] = p.dma("sync", xnt_sc[et, :, :], xnt[et][:],
                                      sem="sp")
        if "noq" not in phases:
            proj_kq(q_sc, wqT, c_cosq, c_sinq, 0, "q")
        if "nokv" not in phases:
            for g in range(2):
                ln_group(xb, g * 2048)
                proj_kq(k_sc, wkT, c_cosk, c_sink, g * 2048, "k")
                proj_v(g * 2048)

        p.wait("vector", "sm", p.cnt["sm"])
        p.wait("vector", "vm", p.cnt["vm"])
        vv0 = p.op("vector", lambda E: E.tensor_tensor(
            nbz[0:1, :], nrm[0:1, 0:H], nrm[0:1, H:2 * H], ALU.mult), sem="vm")
        p.wait("scalar", "vm", vv0)
        sn = p.op("scalar", lambda E:
                  E.activation(nbz[0:1, :], nbz[0:1, :], AF.Sqrt), sem="sm")
        p.wait("vector", "sm", sn)
        vn = p.op("vector", lambda E: E.tensor_scalar(
            nbz[0:1, :], nbz[0:1, :], -1.02 * INV_SQRT_D, None, ALU.mult),
            sem="vm")
        p.wait("tensor", "vm", vn)
        p.bank_acquire(6)
        tn = mm(pb[6][:, 0:H], ones2[:], nbz[:], True, True, sem="tm")
        p.wait("vector", "tm", tn)
        nd = p.op("vector", lambda E: E.tensor_copy(negB[:], pb[6][:, 0:H]),
                  sem="vm")
        p.bank_release(6, "vm", nd)

    off[0] = 0
    oT = [A(f"oT{i}", [128, TOK], BF16) for i in range(H)]
    kbuf = [A(f"kbuf{i}", [128, T], BF16) for i in range(2)]
    vbuf = [A(f"vbuf{i}", [128, 32, 132], BF16) for i in range(2)]
    qbuf = [A(f"qbuf{i}", [128, TOK], BF16) for i in range(2)]
    ptb = [A(f"ptb{i}", [128, 512], BF16) for i in range(4)]
    mk_sb = A("mk_sb", [128, NSLOT * 8 * 512], U8)
    ost = [A(f"ost{i}", [128, 128], BF16) for i in range(2)]
    rcp = A("rcp", [128, 4], F32)
    assert off[0] <= AR

    if "p2" in phases:
        p.barrier()
        p.dma("sync",
              mk_sb.ap().rearrange("p (si mi q) -> p si mi q",
                                   si=NSLOT, mi=8),
              masks.ap(), sem="cs")
        p.wait_all("vector", "cs")
        p.op("vector", lambda E: E.memset(vbuf[0].ap()[:, :, 128:129], 1.0))
        mset = p.op("vector", lambda E: E.memset(vbuf[1].ap()[:, :, 128:129], 1.0),
                    sem="vm")
        p.wait("sync", "vm", mset)

        hl_free = [0, 0]
        ost_free = [0, 0]
        for h in range(H):
            hb = h % 2
            hk = "hk0" if hb == 0 else "hk1"
            p.wait("sync", "tm", hl_free[hb])
            p.dma("sync", kbuf[hb][:], k_sc[h, :, :], sem=hk)
            p.dma("sync", vbuf[hb].ap()[:, :, 0:128], v_sc[h, :, :, :], sem=hk)
            p.dma("sync", qbuf[hb][:], q_sc[h, :, :], sem=hk)
            p.wait_all("tensor", hk)
            last_tm = 0
            pt_free = [0, 0, 0, 0]
            for si in range(NSLOT):
                nkt = 4 * EXT[si]
                qs = si * 512
                for qq in range(4):
                    p.bank_acquire(2 + qq)

                SBANKS = (0, 1, 7)

                def s_mm(kt):
                    sbk = SBANKS[kt % 3]
                    p.bank_acquire(sbk)
                    return mm(pb[sbk][:, :],
                              kbuf[hb][:, kt * 128:(kt + 1) * 128],
                              qbuf[hb][:, qs:qs + 512], True, True, sem="tm")

                tvs = [s_mm(0)]
                if nkt > 1:
                    tvs.append(s_mm(1))
                for kt in range(nkt):
                    if kt + 2 < nkt:
                        tvs.append(s_mm(kt + 2))
                    sbk = SBANKS[kt % 3]
                    slot = kt % 4
                    p.wait("scalar", "tm", pt_free[slot])
                    p.wait("scalar", "tm", tvs[kt])
                    se = p.op("scalar", lambda E, sbk=sbk, slot=slot, h=h:
                              E.activation(ptb[slot][:], pb[sbk][:], AF.Exp,
                                           bias=negB[:, h:h + 1],
                                           scale=INV_SQRT_D), sem="sm")
                    p.bank_release(sbk, "sm", se)
                    if kt >= nkt - 8:
                        mi = kt - (nkt - 8)
                        mo = (si * 8 + mi) * 512
                        p.wait("vector", "sm", se)
                        sv = p.op("vector", lambda E, slot=slot, mo=mo:
                                  E.tensor_mul(ptb[slot][:], ptb[slot][:],
                                               mk_sb[:, mo:mo + 512]), sem="vm")
                        p.wait("tensor", "vm", sv)
                    else:
                        p.wait("tensor", "sm", se)
                    for qq in range(4):
                        v = mm(pb[2 + qq][:, 0:129],
                               ptb[slot][:, qq * 128:(qq + 1) * 128],
                               vbuf[hb].ap()[:, kt, 0:129],
                               kt == 0, kt == nkt - 1,
                               sem="tm" if qq == 3 else None)
                    pt_free[slot] = v
                    last_tm = v
                p.wait("vector", "tm", last_tm)
                for qq in range(4):
                    ob = qq % 2
                    rv = p.op("vector", lambda E, qq=qq:
                              E.reciprocal(rcp[:, qq:qq + 1],
                                           pb[2 + qq][:, 128:129]), sem="vm")
                    p.wait("vector", "vm", rv)
                    p.wait("vector", "tm", ost_free[ob])
                    vo = p.op("vector", lambda E, qq=qq, ob=ob: E.tensor_scalar(
                        ost[ob][:], pb[2 + qq][:, 0:128], rcp[:, qq:qq + 1],
                        None, ALU.mult), sem="vm")
                    p.bank_release(2 + qq, "vm", vo)
                    p.wait("tensor", "vm", vo)
                    p.bank_acquire(6)
                    tt_ = transpose(pbb[6][:, 0:128], ost[ob][:], sem="tm")
                    ost_free[ob] = tt_
                    p.wait("scalar", "tm", tt_)
                    so = p.op("scalar", lambda E, h=h, qs=qs, qq=qq:
                              E.copy(oT[h][:, qs + qq * 128: qs + (qq + 1) * 128],
                                     pbb[6][:, 0:128]), sem="sm")
                    p.bank_release(6, "sm", so)
            hl_free[hb] = last_tm

    off[0] = 64 * 1024
    wo_sb = [A(f"wosb{i}", [128, ET * 512], BF16) for i in range(2)]
    xres = [A(f"xres{i}", [128, 512], F32) for i in range(2)]
    rst = [A(f"rst{i}", [128, 512], F32) for i in range(2)]
    assert off[0] <= AR

    if "po" in phases:
        p.barrier()
        wo_free = [0, 0]
        rs_free = [0, 0]
        xr_free = [0, 0]
        for fc in range(4):
            wb = fc % 2
            p.wait("gpsimd", "tm", wo_free[wb])
            p.dma("gpsimd",
                  wo_sb[wb].ap().rearrange("p (et f) -> p et f", f=512),
                  bass.AP(woT, fc * 512, [[E, 128], [128 * E, ET], [1, 512]]),
                  sem="wg0" if wb == 0 else "wg1")
            p.wait_all("tensor", "wg0" if wb == 0 else "wg1")
            last_mm_w = 0
            for tt in range(16):
                bk = tt % 2
                p.bank_acquire(bk)
                p.wait("sync", "vm", xr_free[bk])
                p.dma("sync", xres[bk][:],
                      xo[tt * 128:(tt + 1) * 128, fc * 512:(fc + 1) * 512],
                      sem="xl0" if bk == 0 else "xl1")
                for et in range(ET):
                    v = mm(pb[bk][:, :], oT[et][:, tt * 128:(tt + 1) * 128],
                           wo_sb[wb][:, et * 512:(et + 1) * 512],
                           et == 0, et == ET - 1,
                           sem="tm" if et == ET - 1 else None)
                last_mm_w = v
                rb = tt % 2
                p.wait_all("vector", "rg")
                p.wait("vector", "tm", v)
                p.wait_all("vector", "xl0" if bk == 0 else "xl1")
                vv = p.op("vector", lambda E, bk=bk, rb=rb:
                          E.tensor_add(rst[rb][:], pb[bk][:], xres[bk][:]),
                          sem="vm")
                p.bank_release(bk, "vm", vv)
                xr_free[bk] = vv
                p.wait("sync", "vm", vv)
                p.dma("sync",
                      r_sc[tt * 128:(tt + 1) * 128, fc * 512:(fc + 1) * 512],
                      rst[rb][:], sem="rg")
            wo_free[wb] = last_mm_w

    off[0] = 0
    xno = [A(f"xno{i}", [128, TOK], BF16) for i in range(ET)]
    gw_sb = [A(f"gwsb{i}", [128, E], BF16) for i in range(2)]
    uw_sb = [A(f"uwsb{i}", [128, E], BF16) for i in range(2)]
    g8 = [A(f"g8_{i}", [128, E], FP8) for i in range(2)]
    u8 = [A(f"u8_{i}", [128, E], FP8) for i in range(2)]
    sgt = A("sgt", [128, 512], F32)
    upt = A("upt", [128, 512], F32)
    fst = [A(f"fst{i}", [128, 512], BF16) for i in range(2)]
    assert off[0] <= AR

    if "p3" in phases:
        p.barrier()
        for et in range(ET):
            p.dma("sync", xno[et][:], xnt_sc[et, :, :], sem="cs")
        p.wait_all("tensor", "cs")
        gw_free = [0, 0]
        fs_free = [0, 0]
        sg_free = [0]
        up_free = [0]
        for it in range(IT):
            wb = it % 2
            wg = "wg0" if wb == 0 else "wg1"
            p.wait("gpsimd", "tm", gw_free[wb])
            p.dma("gpsimd", g8[wb][:],
                  bass.AP(gw, it * 128, [[I, 128], [128 * I, ET], [1, 128]]),
                  sem=wg)
            p.dma("gpsimd", u8[wb][:],
                  bass.AP(uw, it * 128, [[I, 128], [128 * I, ET], [1, 128]]),
                  sem=wg)
            p.wait_all("gpsimd", wg)
            p.dma("gpsimd", gw_sb[wb][:], g8[wb][:], sem=wg)
            p.dma("gpsimd", uw_sb[wb][:], u8[wb][:], sem=wg)
            p.wait_all("tensor", wg)
            last_mm_w = 0
            for tc in range(4):
                bkg = tc % 2
                bku = 2 + tc % 2
                p.bank_acquire(bkg)
                p.bank_acquire(bku)
                for et in range(ET):
                    mm(pb[bkg][:, :], gw_sb[wb][:, et * 128:(et + 1) * 128],
                       xno[et][:, tc * 512:(tc + 1) * 512], et == 0, et == ET - 1)
                for et in range(ET):
                    v = mm(pb[bku][:, :], uw_sb[wb][:, et * 128:(et + 1) * 128],
                           xno[et][:, tc * 512:(tc + 1) * 512],
                           et == 0, et == ET - 1,
                           sem="tm" if et == ET - 1 else None)
                last_mm_w = v
                p.wait("scalar", "tm", v)
                p.wait("scalar", "vm", sg_free[0])
                sg1 = p.op("scalar", lambda E, bkg=bkg, it=it: E.activation(
                    sgt[:], pb[bkg][:], AF.Silu, scale=gs_sb[:, it:it + 1]),
                    sem="sm")
                p.bank_release(bkg, "sm", sg1)
                fb = tc % 2
                p.wait("vector", "sm", sg1)
                p.wait("vector", "vm", up_free[0])
                uv = p.op("vector", lambda E, bku=bku, it=it: E.tensor_scalar(
                    upt[:], pb[bku][:], us_sb[:, it:it + 1], None, ALU.mult),
                    sem="vm")
                p.wait("vector", "vm", uv)
                p.wait_all("vector", "sp")
                vf = p.op("vector", lambda E, fb=fb:
                          E.tensor_mul(fst[fb][:], sgt[:], upt[:]), sem="vm")
                sg_free[0] = vf
                up_free[0] = vf
                p.bank_release(bku, "vm", vf)
                p.wait("sync", "vm", vf)
                for j in range(4):
                    p.dma("sync",
                          fuse_sc[tc * 4 + j, :, it * 128:(it + 1) * 128],
                          fst[fb][:, j * 128:(j + 1) * 128], sem="sp")
            gw_free[wb] = last_mm_w

        dwq = [[nc.alloc_sbuf_tensor_at(f"dwq0_{i}", [128, 512], BF16,
                                        offset=abase + 94 * 1024 + i * 1024)
                for i in range(IT)], None]
        d8b = [nc.alloc_sbuf_tensor_at(f"d8b{i}", [128, 512], FP8,
                                       offset=abase + 143 * 1024 + i * 512)
               for i in range(4)]
        off[0] = 0
        dwq[1] = [A(f"dwq1_{i}", [128, 512], BF16) for i in range(IT)]
        ftb = [A(f"ftb{i}", [128, IT * 128], BF16) for i in range(2)]
        rres = [A(f"rres{i}", [128, 512], F32) for i in range(2)]
        ytmp = [A(f"ytmp{i}", [128, 512], F32) for i in range(2)]
        yst = [A(f"yst{i}", [128, 512], F32) for i in range(2)]
        assert off[0] <= 94 * 1024

        dwq_free = [0, 0]

        def load_quarter(q4):
            par = q4 % 2
            wg = "wg0" if par == 0 else "wg1"
            p.wait("gpsimd", "tm", dwq_free[par])
            for g in range(IT // 4):
                p.wait_all("gpsimd", wg)
                for u in range(4):
                    it = g * 4 + u
                    p.dma("gpsimd", d8b[u][:],
                          dw[it * 128:(it + 1) * 128,
                             q4 * 512:(q4 + 1) * 512], sem="d8s")
                p.wait_all("gpsimd", "d8s")
                for u in range(4):
                    it = g * 4 + u
                    p.dma("gpsimd", dwq[par][it][:], d8b[u][:], sem=wg)

        load_quarter(0)
        p.barrier()
        ft_free = [0, 0]
        rr_free = [0, 0]
        y_pending = None
        for q4 in range(4):
            par = q4 % 2
            if q4 + 1 < 4:
                load_quarter(q4 + 1)
            p.wait_all("tensor", "wg0" if par == 0 else "wg1")
            last_mm = 0
            for tt in range(16):
                fb = tt % 2
                fbs = "fb0" if fb == 0 else "fb1"
                rp = tt % 2
                drs = "dr0" if rp == 0 else "dr1"
                p.wait("sync", "tm", ft_free[fb])
                p.dma("sync", ftb[fb][:], fuse_sc[tt, :, :], sem=fbs)
                p.wait("sync", "vm", rr_free[rp])
                p.dma("sync", rres[rp][:],
                      r_sc[tt * 128:(tt + 1) * 128, q4 * 512:(q4 + 1) * 512],
                      sem=drs)
                if y_pending is not None:
                    pv, ptt, pcs, ppar = y_pending
                    p.wait("sync", "vm", pv)
                    p.dma("sync", y[ptt * 128:(ptt + 1) * 128, pcs],
                          yst[ppar][:], sem="ou0" if ppar == 0 else "ou1")
                    y_pending = None
                p.wait_all("tensor", fbs)
                bk = tt % 4
                p.bank_acquire(bk)
                for it in range(IT):
                    v = mm(pb[bk][:, :], ftb[fb][:, it * 128:(it + 1) * 128],
                           dwq[par][it][:], it == 0, it == IT - 1,
                           sem="tm" if it == IT - 1 else None)
                last_mm = v
                ous = "ou0" if par == 0 else "ou1"
                p.wait_all("vector", ous)
                p.wait("vector", "tm", v)
                p.wait_all("vector", drs)
                cs = slice(q4 * 512, (q4 + 1) * 512)
                mv = p.op("vector", lambda E, bk=bk, cs=cs, par=par:
                          E.tensor_mul(ytmp[par][:], pb[bk][:], dsb[:, cs]),
                          sem="vm")
                p.wait("vector", "vm", mv)
                vv = p.op("vector", lambda E, par=par, rp=rp:
                          E.tensor_add(yst[par][:], ytmp[par][:], rres[rp][:]),
                          sem="vm")
                p.bank_release(bk, "vm", vv)
                rr_free[rp] = vv
                y_pending = (vv, tt, cs, par)
                ft_free[fb] = last_mm
            dwq_free[par] = last_mm
        pv, ptt, pcs, ppar = y_pending
        p.wait("sync", "vm", pv)
        p.dma("sync", y[ptt * 128:(ptt + 1) * 128, pcs], yst[ppar][:],
              sem="ou0" if ppar == 0 else "ou1")

    if "p3" not in phases:
        zout = SB("zout", [128, 512], F32)
        zv = p.op("vector", lambda E: E.memset(zout[:], 0.0), sem="vm")
        p.wait("sync", "vm", zv)
        for tt in range(16):
            for j in range(4):
                p.dma("sync", y[tt * 128:(tt + 1) * 128, j * 512:(j + 1) * 512],
                      zout[:], sem="out")

    p.wait("sync", "out", p.cnt["out"])
    p.wait("sync", "ou0", p.cnt["ou0"])
    p.wait("sync", "ou1", p.cnt["ou1"])

    sem_handles = {s: es.enter_context(nc.semaphore(s))
                   for s in p.cnt}
    p.emit(nc, sem_handles)
    nc._keepalive = es
    nc._dbg = {"oT": [t.name for t in oT], "xnt": [t.name for t in xnt],
               "xno": [t.name for t in xno]}
    return nc


def host_inputs(half):
    cos, sin = rope_tables()
    cosT = np.ascontiguousarray(cos.T)
    sinT = np.ascontiguousarray(sin.T).copy()
    sinT[64:, :] *= -1.0
    rows = own_rows(half)
    mask = np.zeros((NSLOT, 8, 128, 512), np.uint8)
    for si in range(NSLOT):
        nkt = 4 * EXT[si]
        gq = rows[si * 512:(si + 1) * 512]
        for mi in range(8):
            kt = nkt - 8 + mi
            kpos = kt * 128 + np.arange(128)
            mask[si, mi] = (kpos[:, None] <= gq[None, :]).astype(np.uint8)
    return {
        "rows": rows,
        "cosq": np.ascontiguousarray(cosT[:, rows]),
        "sinq": np.ascontiguousarray(sinT[:, rows]),
        "cosk": cosT,
        "sink": np.ascontiguousarray(sinT),
        "masks": np.ascontiguousarray(mask.transpose(2, 0, 1, 3)),
    }


def make_in_map(x_full, half, b, wqT, wkT, wvT, woT, gw, uw, dw, gsv, usv, dsv,
                identb):
    hd = host_inputs(half)
    x_b = np.ascontiguousarray(x_full[b])
    return hd["rows"], {
        "xb": x_b,
        "xo": np.ascontiguousarray(x_b[hd["rows"]]),
        "wqT": wqT, "wkT": wkT, "wvT": wvT, "woT": woT,
        "gw": gw, "uw": uw, "dw": dw,
        "gs": gsv, "us": usv, "ds": dsv,
        "cosk": hd["cosk"], "sink": hd["sink"],
        "cosq": hd["cosq"], "sinq": hd["sinq"],
        "identb": identb, "masks": hd["masks"],
    }


def kernel(L0_attn_x, L0_attn_wq, L0_attn_wk, L0_attn_wv, L0_attn_wo,
           L0_ffn_gate_w, L0_ffn_up_w, L0_ffn_down_w,
           L0_ffn_gate_scale, L0_ffn_up_scale, L0_ffn_down_scale):
    nc = build_nc()
    wqT = np.ascontiguousarray(L0_attn_wq.T)
    wkT = np.ascontiguousarray(L0_attn_wk.T)
    wvT = np.ascontiguousarray(L0_attn_wv.T)
    woT = np.ascontiguousarray(L0_attn_wo.T)
    identb = np.eye(128, dtype=ml_dtypes.bfloat16)
    in_maps, rows = [], {}
    for c in range(NCORE):
        b, half = c // 2, c % 2
        r, im = make_in_map(L0_attn_x, half, b, wqT, wkT, wvT, woT,
                            L0_ffn_gate_w, L0_ffn_up_w, L0_ffn_down_w,
                            L0_ffn_gate_scale, L0_ffn_up_scale,
                            L0_ffn_down_scale, identb)
        rows[half] = r
        in_maps.append(im)
    res = run_bass_kernel_spmd(nc, in_maps, core_ids=list(range(NCORE)))
    out = np.empty((B, T, E), np.float32)
    for c in range(NCORE):
        b, half = c // 2, c % 2
        out[b, rows[half]] = res.results[c]["y"]
    return out


# version 12
# speedup vs baseline: 1.0498x; 1.0217x over previous
import numpy as np
import ml_dtypes
from contextlib import ExitStack

import concourse.bass as bass
import concourse.mybir as mybir
from concourse.bass_utils import run_bass_kernel_spmd

F32 = mybir.dt.float32
BF16 = mybir.dt.bfloat16
FP8 = mybir.dt.float8e4
U8 = mybir.dt.uint8
AF = mybir.ActivationFunctionType
ALU = mybir.AluOpType
AX = mybir.AxisListType

B, T, E = 4, 4096, 2048
H, Dh = 16, 128
I = 6144
NCORE = 8
TOK = 2048
NSLOT = 4
EXT = [8, 6, 4, 2]
CHUNKS = {0: [7, 5, 2, 0], 1: [6, 4, 3, 1]}
ET = E // 128
IT = I // 128
INV_SQRT_D = float(np.float32(1.0) / np.float32(np.sqrt(np.float32(Dh))))
ENGINES = ("sync", "gpsimd", "tensor", "vector", "scalar")
COMPUTE = ("tensor", "vector", "scalar")
DTS = {F32: 4, BF16: 2, U8: 1, FP8: 1}


def own_rows(half):
    return np.concatenate([np.arange(c * 512, (c + 1) * 512) for c in CHUNKS[half]])


_TABLES = None


def rope_tables():
    global _TABLES
    if _TABLES is None:
        f32 = np.float32
        e = np.arange(0, Dh, 2, dtype=f32) / f32(Dh)
        inv = (f32(1.0) / np.power(f32(1e6), e)).astype(f32)
        f = np.arange(T, dtype=f32)[:, None] * inv[None, :]
        cos = np.concatenate([np.cos(f), np.cos(f)], -1).astype(f32)
        sin = np.concatenate([np.sin(f), np.sin(f)], -1).astype(f32)
        _TABLES = (cos, sin)
    return _TABLES


class Prog:

    def __init__(self):
        self.ops = {e: [] for e in ENGINES}
        self.cnt = {k: 0 for k in ("ds", "dg", "tm", "vm", "sm", "out", "cs",
                                    "xl0", "xl1", "wg0", "wg1", "sh", "sp",
                                    "hk0", "hk1", "dr0", "dr1", "fb0", "fb1",
                                    "ou0", "ou1", "rg", "d8s", "wg2")}
        self.bank = {}

    def wait(self, eng, sem, val):
        if val:
            self.ops[eng].append(lambda E, s=sem, v=val: E.wait_ge(self.sem[s], v))

    def wait_all(self, eng, sem):
        self.wait(eng, sem, self.cnt[sem])

    def dma(self, eng, out, in_, sem=None):
        if sem is None:
            sem = "ds" if eng == "sync" else "dg"
        self.cnt[sem] += 16
        v = self.cnt[sem]

        def f(E, out=out, in_=in_, s=sem):
            E.dma_start(out, in_).then_inc(self.sem[s], 16)

        self.ops[eng].append(f)
        return v

    def op(self, eng, fn, sem=None):
        v = None
        if sem is not None:
            self.cnt[sem] += 1
            v = self.cnt[sem]

        def f(E, fn=fn, s=sem):
            inst = fn(E)
            if s is not None:
                inst.then_inc(self.sem[s], 1)

        self.ops[eng].append(f)
        return v

    def bank_acquire(self, bk):
        g = self.bank.get(bk)
        if g:
            self.wait("tensor", g[0], g[1])
            self.bank[bk] = None

    def bank_release(self, bk, sem, val):
        self.bank[bk] = (sem, val)

    def barrier(self, engines=ENGINES):
        for e in engines:
            for s, v in list(self.cnt.items()):
                self.wait(e, s, v)

    def emit(self, nc, sem_handles):
        self.sem = sem_handles
        with nc.Block() as block:
            @block.sync
            def _(e):
                for f in self.ops["sync"]:
                    f(e)

            @block.gpsimd
            def _(e):
                for f in self.ops["gpsimd"]:
                    f(e)

            @block.tensor
            def _(e):
                for f in self.ops["tensor"]:
                    f(e)

            @block.vector
            def _(e):
                for f in self.ops["vector"]:
                    f(e)

            @block.scalar
            def _(e):
                for f in self.ops["scalar"]:
                    f(e)


def build_nc(phases=("p1", "p2", "po", "p3"), dbg=False):
    nc = bass.Bass("TRN2", target_bir_lowering=False,
                   detect_race_conditions=False)
    SCR = "ExternalOutput" if dbg else "Internal"

    xb = nc.dram_tensor("xb", [T, E], F32, kind="ExternalInput")
    xo = nc.dram_tensor("xo", [TOK, E], F32, kind="ExternalInput")
    wqT = nc.dram_tensor("wqT", [E, E], F32, kind="ExternalInput")
    wkT = nc.dram_tensor("wkT", [E, E], F32, kind="ExternalInput")
    wvT = nc.dram_tensor("wvT", [E, E], F32, kind="ExternalInput")
    woT = nc.dram_tensor("woT", [E, E], F32, kind="ExternalInput")
    gw = nc.dram_tensor("gw", [E, I], F32, kind="ExternalInput")
    uw = nc.dram_tensor("uw", [E, I], F32, kind="ExternalInput")
    dw = nc.dram_tensor("dw", [I, E], F32, kind="ExternalInput")
    gs = nc.dram_tensor("gs", [I], F32, kind="ExternalInput")
    us = nc.dram_tensor("us", [I], F32, kind="ExternalInput")
    ds_ = nc.dram_tensor("ds", [E], F32, kind="ExternalInput")
    cosk = nc.dram_tensor("cosk", [128, T], F32, kind="ExternalInput")
    sink = nc.dram_tensor("sink", [128, T], F32, kind="ExternalInput")
    cosq = nc.dram_tensor("cosq", [128, TOK], F32, kind="ExternalInput")
    sinq = nc.dram_tensor("sinq", [128, TOK], F32, kind="ExternalInput")
    identb = nc.dram_tensor("identb", [128, 128], BF16, kind="ExternalInput")
    masks = nc.dram_tensor("masks", [128, NSLOT, 8, 512], U8, kind="ExternalInput")

    k_sc = nc.dram_tensor("k_sc", [H, 128, T], BF16, kind=SCR)
    v_sc = nc.dram_tensor("v_sc", [H, 128, T // 128, 128], BF16, kind=SCR)
    q_sc = nc.dram_tensor("q_sc", [H, 128, TOK], BF16, kind=SCR)
    xnt_sc = nc.dram_tensor("xnt_sc", [ET, 128, TOK], BF16, kind=SCR)
    fuse_sc = nc.dram_tensor("fuse_sc", [TOK // 128, 128, IT * 128], BF16, kind=SCR)
    r_sc = nc.dram_tensor("r_sc", [TOK, E], F32, kind=SCR)
    y = nc.dram_tensor("y", [TOK, E], F32, kind="ExternalOutput")

    es = ExitStack()
    SB = lambda nm, sh, dt: es.enter_context(nc.sbuf_tensor(nm, sh, dt))
    const_idb = SB("c_idb", [128, 128], BF16)
    negB = SB("negB", [128, H], F32)
    gs_sb = SB("gs_sb", [128, IT], F32)
    us_sb = SB("us_sb", [128, IT], F32)
    dsb = SB("dsb", [128, E], F32)
    ones_b = SB("ones_b", [128, 1], BF16)
    ones2 = SB("ones2", [128, 128], F32)
    nrm = SB("nrm", [1, 2 * H], F32)
    nbz = SB("nbz", [128, H], F32)

    pb = [es.enter_context(nc.psum_tensor(f"pb{i}", [128, 512], F32))
          for i in range(8)]
    pbb = [t.bitcast(BF16) for t in pb]

    AR = 189 * 1024
    arena = nc.alloc_sbuf_tensor("arena", [128, AR], mybir.dt.uint8)
    abase = nc.lookup_mloc(arena).addr
    off = [0]

    def A(nm, sh, dt):
        nbytes = int(np.prod(sh[1:])) * DTS[dt]
        o = (off[0] + 31) // 32 * 32
        t = nc.alloc_sbuf_tensor_at(nm, sh, dt, offset=abase + o)
        off[0] = o + nbytes
        assert off[0] <= AR, f"arena overflow at {nm}: {off[0]}"
        return t

    p = Prog()

    def mm(psum_ap, lhsT, rhs, start, stop, sem=None):
        return p.op("tensor",
                    lambda E, o=psum_ap, l=lhsT, r=rhs, a=start, b=stop:
                    E.matmul(o, l, r, start=a, stop=b), sem=sem)

    def transpose(psum_ap, in_ap, sem=None):
        return p.op("tensor",
                    lambda E, o=psum_ap, i=in_ap:
                    E.transpose(o, i, const_idb[:]), sem=sem)

    c_cosk = A("c_cosk", [128, T], F32)
    c_sink = A("c_sink", [128, T], F32)
    c_cosq = A("c_cosq", [128, TOK], F32)
    c_sinq = A("c_sinq", [128, TOK], F32)
    _off_tab = off[0]
    dz = A("dz", [128, E], F32)
    off[0] = _off_tab

    p.dma("sync", c_cosk[:], cosk[:], sem="cs")
    p.dma("sync", c_sink[:], sink[:], sem="cs")
    p.dma("sync", c_cosq[:], cosq[:], sem="cs")
    p.dma("sync", c_sinq[:], sinq[:], sem="cs")
    id_ld = p.dma("sync", const_idb[:], identb[:], sem="cs")
    for j in range(IT):
        p.dma("sync", gs_sb[:, j:j + 1], bass.AP(gs, j * 128, [[1, 128], [1, 1]]),
              sem="cs")
        p.dma("sync", us_sb[:, j:j + 1], bass.AP(us, j * 128, [[1, 128], [1, 1]]),
              sem="cs")
    p.op("vector", lambda E: E.memset(ones_b[:], 1.0))
    p.op("vector", lambda E: E.memset(ones2[:], 1.0))
    p.op("vector", lambda E: E.memset(nbz[:], 0.0))
    zd = p.op("vector", lambda E: E.memset(dz[:], 0.0), sem="vm")
    p.op("vector", lambda E: E.memset(nrm[:], 0.0), sem="vm")
    p.wait("sync", "vm", zd)
    p.dma("sync", dz[0:1, :], bass.AP(ds_, 0, [[E, 1], [1, E]]), sem="cs")
    p.wait_all("tensor", "cs")
    p.wait("tensor", "vm", zd)
    for j in range(4):
        p.bank_acquire(7)
        tv = mm(pb[7][:, :], ones2[:], dz[:, j * 512:(j + 1) * 512],
                True, True, sem="tm")
        p.wait("vector", "tm", tv)
        ev = p.op("vector",
                  lambda E, j=j: E.tensor_copy(dsb[:, j * 512:(j + 1) * 512],
                                               pb[7][:, :]),
                  sem="vm")
        p.bank_release(7, "vm", ev)

    xnt = [A(f"xnt{i}", [128, 2048], BF16) for i in range(ET)]
    x_sb = [A(f"xsb{i}", [128, E], F32) for i in range(2)]
    sqb = A("sqb", [128, E], F32)
    xnb = [A(f"xnb{i}", [128, E], BF16) for i in range(2)]
    w_sb = [A(f"wsb{i}", [128, E], BF16) for i in range(2)]
    w_sb.append(nc.alloc_sbuf_tensor_at(
        "wsb2", [128, E], BF16, offset=nc.lookup_mloc(xnb[0]).addr))
    _u0 = (off[0] + 31) // 32 * 32
    kcw = A("kcw", [128, 2048], F32)
    kww = A("kww", [128, 2048], F32)
    rshw = A("rshw", [128, 2048], F32)
    rbw = A("rbw", [128, 2048], BF16)
    sqw = [A(f"sqw{i}", [128, 2048], BF16) for i in range(2)]
    wv_sb = nc.alloc_sbuf_tensor_at("wvsb", [128, ET * 512], BF16,
                                    offset=abase + _u0)
    vst = [nc.alloc_sbuf_tensor_at(f"vst{i}", [128, 512], BF16,
                                   offset=abase + _u0 + 16 * 1024 + i * 1024)
           for i in range(4)]
    stat = [A(f"stat{i}", [128, 4], F32) for i in range(2)]
    assert off[0] <= AR
    E_ = float(E)

    xnt_spill = [0] * ET

    if "p1" in phases:
        ln_x_free = [0, 0]
        xnb_free = [0, 0]
        sq_free = [0]

        def ln_group(src, t0):
            p.wait_all("tensor", "cs")
            p.wait("scalar", "tm", p.cnt["tm"])
            for tb in range(16):
                buf = tb % 2
                p.wait("sync", "sm", ln_x_free[buf])
                p.dma("sync", x_sb[buf][:],
                      src[t0 + tb * 128: t0 + (tb + 1) * 128, :],
                      sem="xl0" if buf == 0 else "xl1")
                p.wait_all("vector", "xl0" if buf == 0 else "xl1")
                r1 = p.op("vector", lambda E, b=buf:
                          E.reduce_sum(stat[b][:, 0:1], x_sb[b][:], axis=AX.X),
                          sem="vm")
                p.wait("vector", "vm", r1)
                r2 = p.op("vector", lambda E, b=buf: E.tensor_scalar(
                    stat[b][:, 1:2], stat[b][:, 0:1], 1.0 / E_, None, ALU.mult),
                    sem="vm")
                p.wait("vector", "vm", r2)
                vsub = p.op("vector", lambda E, b=buf: E.tensor_scalar(
                    x_sb[b][:], x_sb[b][:], stat[b][:, 1:2], None, ALU.subtract),
                    sem="vm")
                p.wait("scalar", "vm", vsub)
                p.wait("scalar", "vm", sq_free[0])
                ssq = p.op("scalar", lambda E, b=buf:
                           E.activation(sqb[:], x_sb[b][:], AF.Square), sem="sm")
                p.wait("vector", "sm", ssq)
                vq = p.op("vector", lambda E, b=buf:
                          E.reduce_sum(stat[b][:, 2:3], sqb[:], axis=AX.X), sem="vm")
                sq_free[0] = vq
                p.wait("vector", "vm", vq)
                vv_ = p.op("vector", lambda E, b=buf: E.tensor_scalar(
                    stat[b][:, 2:3], stat[b][:, 2:3], 1.0 / E_, 1e-5,
                    ALU.mult, ALU.add), sem="vm")
                p.wait("scalar", "vm", vv_)
                sq_ = p.op("scalar", lambda E, b=buf:
                           E.activation(stat[b][:, 3:4], stat[b][:, 2:3], AF.Sqrt),
                           sem="sm")
                p.wait("vector", "sm", sq_)
                vrec = p.op("vector", lambda E, b=buf:
                            E.reciprocal(stat[b][:, 3:4], stat[b][:, 3:4]), sem="vm")
                p.wait("scalar", "vm", vrec)
                p.wait("scalar", "tm", xnb_free[buf])
                sxn = p.op("scalar", lambda E, b=buf: E.activation(
                    xnb[b][:], x_sb[b][:], AF.Copy, scale=stat[b][:, 3:4]), sem="sm")
                ln_x_free[buf] = sxn
                p.wait("tensor", "sm", sxn)
                for et in range(ET):
                    bk = et % 2
                    p.bank_acquire(bk)
                    tv = transpose(pbb[bk][:, 0:128],
                                   xnb[buf][:, et * 128:(et + 1) * 128], sem="tm")
                    if et >= ET - 2:
                        xnb_free[buf] = tv
                    p.wait("vector", "tm", tv)
                    ev = p.op("vector", lambda E, et=et, tb=tb, bk=bk:
                              E.tensor_copy(xnt[et][:, tb * 128:(tb + 1) * 128],
                                            pbb[bk][:, 0:128]), sem="vm")
                    p.bank_release(bk, "vm", ev)

        def proj_kq(dst_sc, wT_dram, cos_sb, sin_sb, t0, name):
            p.wait("tensor", "vm", p.cnt["vm"])
            p.wait("sync", "sp", p.cnt["sp"])
            nrm_off = 0 if name == "q" else H
            wbuf_free = [p.cnt["tm"], p.cnt["tm"], p.cnt["tm"]]
            rope_free = [0]
            spill_sq = [0, 0]
            norm_last = [0, 0]
            pend_norm = []
            for ft in range(H):
                wb = ft % 3
                wg = ("wg0", "wg1", "wg2")[wb]
                p.wait("gpsimd", "tm", wbuf_free[wb])
                p.dma("gpsimd", w_sb[wb][:],
                      bass.AP(wT_dram, ft * 128,
                              [[E, 128], [128 * E, ET], [1, 128]]),
                      sem=wg)
                p.wait_all("tensor", wg)
                last_mm_w = 0
                kwv = 0
                for tc in range(4):
                    bk = tc
                    p.bank_acquire(bk)
                    for et in range(ET):
                        v = mm(pb[bk][:, :], w_sb[wb][:, et * 128:(et + 1) * 128],
                               xnt[et][:, tc * 512:(tc + 1) * 512],
                               et == 0, et == ET - 1,
                               sem="tm" if et == ET - 1 else None)
                    last_mm_w = v
                    cs = slice(t0 + tc * 512, t0 + (tc + 1) * 512)
                    ws = slice(tc * 512, (tc + 1) * 512)
                    p.wait("vector", "tm", v)
                    if tc == 0:
                        p.wait("vector", "vm", rope_free[0])
                        p.wait_all("vector", "sp")
                        p.wait("vector", "sm", spill_sq[1])
                    p.op("vector", lambda E, bk=bk, cs=cs, ws=ws, c=cos_sb:
                         E.tensor_mul(kcw[:, ws], pb[bk][:], c[:, cs]))
                    kwv = p.op("vector", lambda E, bk=bk, cs=cs, ws=ws, s=sin_sb:
                               E.tensor_mul(kww[:, ws], pb[bk][:], s[:, cs]),
                               sem="vm")
                    p.bank_release(bk, "vm", kwv)
                p.wait("scalar", "vm", kwv)
                p.dma("scalar", rshw[0:64, :], kww[64:128, :], sem="sh")
                p.dma("scalar", rshw[64:128, :], kww[0:64, :], sem="sh")
                p.wait_all("vector", "sh")
                vfin = p.op("vector", lambda E:
                            E.tensor_add(rbw[:], kcw[:], rshw[:]), sem="vm")
                rope_free[0] = vfin
                p.wait("sync", "vm", vfin)
                p.dma("sync", dst_sc[ft, :, t0:t0 + 2048], rbw[:], sem="sp")
                sb2 = ft % 2
                p.wait("scalar", "vm", vfin)
                p.wait("scalar", "tm", norm_last[sb2])
                ssq = p.op("scalar", lambda E, sb2=sb2:
                           E.activation(sqw[sb2][:], rbw[:], AF.Square), sem="sm")
                spill_sq[1] = ssq
                pend_norm.append((ft, sb2, ssq))
                if len(pend_norm) > 1 or ft == H - 1:
                    todo = pend_norm[:-1] if ft < H - 1 else pend_norm
                    del pend_norm[:len(todo)]
                    for (ftn, sbn, ssqn) in todo:
                        p.wait("tensor", "sm", ssqn)
                        for j in range(4):
                            nb = 4 + j % 2
                            p.bank_acquire(nb)
                            tnv = mm(pb[nb][0:1, 0:512], ones_b[:],
                                     sqw[sbn][:, j * 512:(j + 1) * 512],
                                     True, True, sem="tm")
                            p.wait("vector", "tm", tnv)
                            rmx = p.op("vector", lambda E, nb=nb:
                                       E.reduce_max(stat[0][0:1, 0:1],
                                                    pb[nb][0:1, 0:512],
                                                    axis=AX.X), sem="vm")
                            p.wait("vector", "vm", rmx)
                            nv = p.op("vector", lambda E, ftn=ftn, no=nrm_off:
                                      E.tensor_tensor(
                                          nrm[0:1, no + ftn:no + ftn + 1],
                                          nrm[0:1, no + ftn:no + ftn + 1],
                                          stat[0][0:1, 0:1], ALU.max),
                                      sem="vm")
                            p.bank_release(nb, "vm", nv)
                        norm_last[sbn] = tnv
                wbuf_free[wb] = last_mm_w

        def proj_v(t0):
            p.wait("tensor", "vm", p.cnt["vm"])
            rb_free = [0, 0, 0, 0]
            for fc in range(4):
                p.wait("gpsimd", "tm", p.cnt["tm"])
                p.wait("gpsimd", "vm", p.cnt["vm"])
                p.wait("gpsimd", "sm", p.cnt["sm"])
                p.dma("gpsimd",
                      wv_sb.ap().rearrange("p (et f) -> p et f", f=512),
                      bass.AP(wvT, fc * 512,
                              [[E, 128], [128 * E, ET], [1, 512]]),
                      sem="wg0" if fc % 2 == 0 else "wg1")
                p.wait_all("tensor", "wg0" if fc % 2 == 0 else "wg1")
                for tt in range(16):
                    bk = 2 + tt % 2
                    p.bank_acquire(bk)
                    for et in range(ET):
                        v = mm(pb[bk][:, :], xnt[et][:, tt * 128:(tt + 1) * 128],
                               wv_sb[:, et * 512:(et + 1) * 512],
                               et == 0, et == ET - 1,
                               sem="tm" if et == ET - 1 else None)
                    rb = tt % 4
                    p.wait("scalar", "tm", v)
                    p.wait_all("scalar", "sp")
                    sv = p.op("scalar", lambda E, bk=bk, rb=rb:
                              E.copy(vst[rb][:], pb[bk][:]), sem="sm")
                    p.bank_release(bk, "sm", sv)
                    p.wait("sync", "sm", sv)
                    kt_g = t0 // 128 + tt
                    for j in range(4):
                        p.dma("sync", v_sc[fc * 4 + j, :, kt_g, :],
                              vst[rb][:, j * 128:(j + 1) * 128], sem="sp")

        if "noln" not in phases:
            ln_group(xo, 0)
            p.wait("sync", "vm", p.cnt["vm"])
            for et in range(ET):
                xnt_spill[et] = p.dma("sync", xnt_sc[et, :, :], xnt[et][:],
                                      sem="sp")
        if "noq" not in phases:
            proj_kq(q_sc, wqT, c_cosq, c_sinq, 0, "q")
        if "nokv" not in phases:
            for g in range(2):
                ln_group(xb, g * 2048)
                proj_kq(k_sc, wkT, c_cosk, c_sink, g * 2048, "k")
                proj_v(g * 2048)

        p.wait("vector", "sm", p.cnt["sm"])
        p.wait("vector", "vm", p.cnt["vm"])
        vv0 = p.op("vector", lambda E: E.tensor_tensor(
            nbz[0:1, :], nrm[0:1, 0:H], nrm[0:1, H:2 * H], ALU.mult), sem="vm")
        p.wait("scalar", "vm", vv0)
        sn = p.op("scalar", lambda E:
                  E.activation(nbz[0:1, :], nbz[0:1, :], AF.Sqrt), sem="sm")
        p.wait("vector", "sm", sn)
        vn = p.op("vector", lambda E: E.tensor_scalar(
            nbz[0:1, :], nbz[0:1, :], -1.02 * INV_SQRT_D, None, ALU.mult),
            sem="vm")
        p.wait("tensor", "vm", vn)
        p.bank_acquire(6)
        tn = mm(pb[6][:, 0:H], ones2[:], nbz[:], True, True, sem="tm")
        p.wait("vector", "tm", tn)
        nd = p.op("vector", lambda E: E.tensor_copy(negB[:], pb[6][:, 0:H]),
                  sem="vm")
        p.bank_release(6, "vm", nd)

    off[0] = 0
    oT = [A(f"oT{i}", [128, TOK], BF16) for i in range(H)]
    kbuf = [A(f"kbuf{i}", [128, T], BF16) for i in range(2)]
    vbuf = [A(f"vbuf{i}", [128, 32, 132], BF16) for i in range(2)]
    qbuf = [A(f"qbuf{i}", [128, TOK], BF16) for i in range(2)]
    ptb = [A(f"ptb{i}", [128, 512], BF16) for i in range(4)]
    mk_sb = A("mk_sb", [128, NSLOT * 8 * 512], U8)
    ost = [A(f"ost{i}", [128, 128], BF16) for i in range(2)]
    rcp = A("rcp", [128, 4], F32)
    assert off[0] <= AR

    if "p2" in phases:
        p.barrier()
        p.dma("sync",
              mk_sb.ap().rearrange("p (si mi q) -> p si mi q",
                                   si=NSLOT, mi=8),
              masks.ap(), sem="cs")
        p.wait_all("vector", "cs")
        p.op("vector", lambda E: E.memset(vbuf[0].ap()[:, :, 128:129], 1.0))
        mset = p.op("vector", lambda E: E.memset(vbuf[1].ap()[:, :, 128:129], 1.0),
                    sem="vm")
        p.wait("sync", "vm", mset)

        hl_free = [0, 0]
        ost_free = [0, 0]
        for h in range(H):
            hb = h % 2
            hk = "hk0" if hb == 0 else "hk1"
            p.wait("sync", "tm", hl_free[hb])
            p.dma("sync", kbuf[hb][:], k_sc[h, :, :], sem=hk)
            p.dma("sync", vbuf[hb].ap()[:, :, 0:128], v_sc[h, :, :, :], sem=hk)
            p.dma("sync", qbuf[hb][:], q_sc[h, :, :], sem=hk)
            p.wait_all("tensor", hk)
            last_tm = 0
            pt_free = [0, 0, 0, 0]
            for si in range(NSLOT):
                nkt = 4 * EXT[si]
                qs = si * 512
                for qq in range(4):
                    p.bank_acquire(2 + qq)

                SBANKS = (0, 1, 7)

                def s_mm(kt):
                    sbk = SBANKS[kt % 3]
                    p.bank_acquire(sbk)
                    return mm(pb[sbk][:, :],
                              kbuf[hb][:, kt * 128:(kt + 1) * 128],
                              qbuf[hb][:, qs:qs + 512], True, True, sem="tm")

                tvs = [s_mm(0)]
                if nkt > 1:
                    tvs.append(s_mm(1))
                for kt in range(nkt):
                    if kt + 2 < nkt:
                        tvs.append(s_mm(kt + 2))
                    sbk = SBANKS[kt % 3]
                    slot = kt % 4
                    p.wait("scalar", "tm", pt_free[slot])
                    p.wait("scalar", "tm", tvs[kt])
                    se = p.op("scalar", lambda E, sbk=sbk, slot=slot, h=h:
                              E.activation(ptb[slot][:], pb[sbk][:], AF.Exp,
                                           bias=negB[:, h:h + 1],
                                           scale=INV_SQRT_D), sem="sm")
                    p.bank_release(sbk, "sm", se)
                    if kt >= nkt - 8:
                        mi = kt - (nkt - 8)
                        mo = (si * 8 + mi) * 512
                        p.wait("vector", "sm", se)
                        sv = p.op("vector", lambda E, slot=slot, mo=mo:
                                  E.tensor_mul(ptb[slot][:], ptb[slot][:],
                                               mk_sb[:, mo:mo + 512]), sem="vm")
                        p.wait("tensor", "vm", sv)
                    else:
                        p.wait("tensor", "sm", se)
                    for qq in range(4):
                        v = mm(pb[2 + qq][:, 0:129],
                               ptb[slot][:, qq * 128:(qq + 1) * 128],
                               vbuf[hb].ap()[:, kt, 0:129],
                               kt == 0, kt == nkt - 1,
                               sem="tm" if qq == 3 else None)
                    pt_free[slot] = v
                    last_tm = v
                p.wait("vector", "tm", last_tm)
                for qq in range(4):
                    ob = qq % 2
                    rv = p.op("vector", lambda E, qq=qq:
                              E.reciprocal(rcp[:, qq:qq + 1],
                                           pb[2 + qq][:, 128:129]), sem="vm")
                    p.wait("vector", "vm", rv)
                    p.wait("vector", "tm", ost_free[ob])
                    vo = p.op("vector", lambda E, qq=qq, ob=ob: E.tensor_scalar(
                        ost[ob][:], pb[2 + qq][:, 0:128], rcp[:, qq:qq + 1],
                        None, ALU.mult), sem="vm")
                    p.bank_release(2 + qq, "vm", vo)
                    p.wait("tensor", "vm", vo)
                    p.bank_acquire(6)
                    tt_ = transpose(pbb[6][:, 0:128], ost[ob][:], sem="tm")
                    ost_free[ob] = tt_
                    p.wait("scalar", "tm", tt_)
                    so = p.op("scalar", lambda E, h=h, qs=qs, qq=qq:
                              E.copy(oT[h][:, qs + qq * 128: qs + (qq + 1) * 128],
                                     pbb[6][:, 0:128]), sem="sm")
                    p.bank_release(6, "sm", so)
            hl_free[hb] = last_tm

    off[0] = 64 * 1024
    wo_sb = [A(f"wosb{i}", [128, ET * 512], BF16) for i in range(2)]
    xres = [A(f"xres{i}", [128, 512], F32) for i in range(2)]
    rst = [A(f"rst{i}", [128, 512], F32) for i in range(2)]
    assert off[0] <= AR

    if "po" in phases:
        p.barrier()
        wo_free = [0, 0]
        rs_free = [0, 0]
        xr_free = [0, 0]
        for fc in range(4):
            wb = fc % 2
            p.wait("gpsimd", "tm", wo_free[wb])
            p.dma("gpsimd",
                  wo_sb[wb].ap().rearrange("p (et f) -> p et f", f=512),
                  bass.AP(woT, fc * 512, [[E, 128], [128 * E, ET], [1, 512]]),
                  sem="wg0" if wb == 0 else "wg1")
            p.wait_all("tensor", "wg0" if wb == 0 else "wg1")
            last_mm_w = 0
            for tt in range(16):
                bk = tt % 2
                p.bank_acquire(bk)
                p.wait("sync", "vm", xr_free[bk])
                p.dma("sync", xres[bk][:],
                      xo[tt * 128:(tt + 1) * 128, fc * 512:(fc + 1) * 512],
                      sem="xl0" if bk == 0 else "xl1")
                for et in range(ET):
                    v = mm(pb[bk][:, :], oT[et][:, tt * 128:(tt + 1) * 128],
                           wo_sb[wb][:, et * 512:(et + 1) * 512],
                           et == 0, et == ET - 1,
                           sem="tm" if et == ET - 1 else None)
                last_mm_w = v
                rb = tt % 2
                p.wait_all("vector", "rg")
                p.wait("vector", "tm", v)
                p.wait_all("vector", "xl0" if bk == 0 else "xl1")
                vv = p.op("vector", lambda E, bk=bk, rb=rb:
                          E.tensor_add(rst[rb][:], pb[bk][:], xres[bk][:]),
                          sem="vm")
                p.bank_release(bk, "vm", vv)
                xr_free[bk] = vv
                p.wait("sync", "vm", vv)
                p.dma("sync",
                      r_sc[tt * 128:(tt + 1) * 128, fc * 512:(fc + 1) * 512],
                      rst[rb][:], sem="rg")
            wo_free[wb] = last_mm_w

    off[0] = 0
    xno = [A(f"xno{i}", [128, TOK], BF16) for i in range(ET)]
    gw_sb = [A(f"gwsb{i}", [128, E], BF16) for i in range(2)]
    uw_sb = [A(f"uwsb{i}", [128, E], BF16) for i in range(2)]
    g8 = [A(f"g8_{i}", [128, E], FP8) for i in range(2)]
    u8 = [A(f"u8_{i}", [128, E], FP8) for i in range(2)]
    sgt = A("sgt", [128, 512], F32)
    upt = A("upt", [128, 512], F32)
    fst = [A(f"fst{i}", [128, 512], BF16) for i in range(2)]
    assert off[0] <= AR

    if "p3" in phases:
        p.barrier()
        for et in range(ET):
            p.dma("sync", xno[et][:], xnt_sc[et, :, :], sem="cs")
        p.wait_all("tensor", "cs")
        gw_free = [0, 0]
        fs_free = [0, 0]
        sg_free = [0]
        up_free = [0]
        for it in range(IT):
            wb = it % 2
            wg = "wg0" if wb == 0 else "wg1"
            p.wait("gpsimd", "tm", gw_free[wb])
            p.dma("gpsimd", g8[wb][:],
                  bass.AP(gw, it * 128, [[I, 128], [128 * I, ET], [1, 128]]),
                  sem=wg)
            p.dma("gpsimd", u8[wb][:],
                  bass.AP(uw, it * 128, [[I, 128], [128 * I, ET], [1, 128]]),
                  sem=wg)
            p.wait_all("gpsimd", wg)
            p.dma("gpsimd", gw_sb[wb][:], g8[wb][:], sem=wg)
            p.dma("gpsimd", uw_sb[wb][:], u8[wb][:], sem=wg)
            p.wait_all("tensor", wg)
            last_mm_w = 0
            for tc in range(4):
                bkg = tc % 2
                bku = 2 + tc % 2
                p.bank_acquire(bkg)
                p.bank_acquire(bku)
                for et in range(ET):
                    mm(pb[bkg][:, :], gw_sb[wb][:, et * 128:(et + 1) * 128],
                       xno[et][:, tc * 512:(tc + 1) * 512], et == 0, et == ET - 1)
                for et in range(ET):
                    v = mm(pb[bku][:, :], uw_sb[wb][:, et * 128:(et + 1) * 128],
                           xno[et][:, tc * 512:(tc + 1) * 512],
                           et == 0, et == ET - 1,
                           sem="tm" if et == ET - 1 else None)
                last_mm_w = v
                p.wait("scalar", "tm", v)
                p.wait("scalar", "vm", sg_free[0])
                sg1 = p.op("scalar", lambda E, bkg=bkg, it=it: E.activation(
                    sgt[:], pb[bkg][:], AF.Silu, scale=gs_sb[:, it:it + 1]),
                    sem="sm")
                p.bank_release(bkg, "sm", sg1)
                fb = tc % 2
                p.wait("vector", "sm", sg1)
                p.wait("vector", "vm", up_free[0])
                uv = p.op("vector", lambda E, bku=bku, it=it: E.tensor_scalar(
                    upt[:], pb[bku][:], us_sb[:, it:it + 1], None, ALU.mult),
                    sem="vm")
                p.wait("vector", "vm", uv)
                p.wait_all("vector", "sp")
                vf = p.op("vector", lambda E, fb=fb:
                          E.tensor_mul(fst[fb][:], sgt[:], upt[:]), sem="vm")
                sg_free[0] = vf
                up_free[0] = vf
                p.bank_release(bku, "vm", vf)
                p.wait("sync", "vm", vf)
                for j in range(4):
                    p.dma("sync",
                          fuse_sc[tc * 4 + j, :, it * 128:(it + 1) * 128],
                          fst[fb][:, j * 128:(j + 1) * 128], sem="sp")
            gw_free[wb] = last_mm_w

        dwq = [[nc.alloc_sbuf_tensor_at(f"dwq0_{i}", [128, 512], BF16,
                                        offset=abase + 94 * 1024 + i * 1024)
                for i in range(IT)], None]
        d8b = [nc.alloc_sbuf_tensor_at(f"d8b{i}", [128, 512], FP8,
                                       offset=abase + 143 * 1024 + i * 512)
               for i in range(4)]
        off[0] = 0
        dwq[1] = [A(f"dwq1_{i}", [128, 512], BF16) for i in range(IT)]
        ftb = [A(f"ftb{i}", [128, IT * 128], BF16) for i in range(2)]
        rres = [A(f"rres{i}", [128, 512], F32) for i in range(2)]
        ytmp = [A(f"ytmp{i}", [128, 512], F32) for i in range(2)]
        yst = [A(f"yst{i}", [128, 512], F32) for i in range(2)]
        assert off[0] <= 94 * 1024

        dwq_free = [0, 0]

        def load_quarter(q4):
            par = q4 % 2
            wg = "wg0" if par == 0 else "wg1"
            p.wait("gpsimd", "tm", dwq_free[par])
            for g in range(IT // 4):
                p.wait_all("gpsimd", wg)
                for u in range(4):
                    it = g * 4 + u
                    p.dma("gpsimd", d8b[u][:],
                          dw[it * 128:(it + 1) * 128,
                             q4 * 512:(q4 + 1) * 512], sem="d8s")
                p.wait_all("gpsimd", "d8s")
                for u in range(4):
                    it = g * 4 + u
                    p.dma("gpsimd", dwq[par][it][:], d8b[u][:], sem=wg)

        load_quarter(0)
        p.barrier()
        ft_free = [0, 0]
        rr_free = [0, 0]
        y_pending = None
        for q4 in range(4):
            par = q4 % 2
            if q4 + 1 < 4:
                load_quarter(q4 + 1)
            p.wait_all("tensor", "wg0" if par == 0 else "wg1")
            last_mm = 0
            for tt in range(16):
                fb = tt % 2
                fbs = "fb0" if fb == 0 else "fb1"
                rp = tt % 2
                drs = "dr0" if rp == 0 else "dr1"
                p.wait("sync", "tm", ft_free[fb])
                p.dma("sync", ftb[fb][:], fuse_sc[tt, :, :], sem=fbs)
                p.wait("sync", "vm", rr_free[rp])
                p.dma("sync", rres[rp][:],
                      r_sc[tt * 128:(tt + 1) * 128, q4 * 512:(q4 + 1) * 512],
                      sem=drs)
                if y_pending is not None:
                    pv, ptt, pcs, ppar = y_pending
                    p.wait("sync", "vm", pv)
                    p.dma("sync", y[ptt * 128:(ptt + 1) * 128, pcs],
                          yst[ppar][:], sem="ou0" if ppar == 0 else "ou1")
                    y_pending = None
                p.wait_all("tensor", fbs)
                bk = tt % 4
                p.bank_acquire(bk)
                for it in range(IT):
                    v = mm(pb[bk][:, :], ftb[fb][:, it * 128:(it + 1) * 128],
                           dwq[par][it][:], it == 0, it == IT - 1,
                           sem="tm" if it == IT - 1 else None)
                last_mm = v
                ous = "ou0" if par == 0 else "ou1"
                p.wait_all("vector", ous)
                p.wait("vector", "tm", v)
                p.wait_all("vector", drs)
                cs = slice(q4 * 512, (q4 + 1) * 512)
                mv = p.op("vector", lambda E, bk=bk, cs=cs, par=par:
                          E.tensor_mul(ytmp[par][:], pb[bk][:], dsb[:, cs]),
                          sem="vm")
                p.wait("vector", "vm", mv)
                vv = p.op("vector", lambda E, par=par, rp=rp:
                          E.tensor_add(yst[par][:], ytmp[par][:], rres[rp][:]),
                          sem="vm")
                p.bank_release(bk, "vm", vv)
                rr_free[rp] = vv
                y_pending = (vv, tt, cs, par)
                ft_free[fb] = last_mm
            dwq_free[par] = last_mm
        pv, ptt, pcs, ppar = y_pending
        p.wait("sync", "vm", pv)
        p.dma("sync", y[ptt * 128:(ptt + 1) * 128, pcs], yst[ppar][:],
              sem="ou0" if ppar == 0 else "ou1")

    if "p3" not in phases:
        zout = SB("zout", [128, 512], F32)
        zv = p.op("vector", lambda E: E.memset(zout[:], 0.0), sem="vm")
        p.wait("sync", "vm", zv)
        for tt in range(16):
            for j in range(4):
                p.dma("sync", y[tt * 128:(tt + 1) * 128, j * 512:(j + 1) * 512],
                      zout[:], sem="out")

    p.wait("sync", "out", p.cnt["out"])
    p.wait("sync", "ou0", p.cnt["ou0"])
    p.wait("sync", "ou1", p.cnt["ou1"])

    sem_handles = {s: es.enter_context(nc.semaphore(s))
                   for s in p.cnt}
    p.emit(nc, sem_handles)
    nc._keepalive = es
    nc._dbg = {"oT": [t.name for t in oT], "xnt": [t.name for t in xnt],
               "xno": [t.name for t in xno]}
    return nc


def host_inputs(half):
    cos, sin = rope_tables()
    cosT = np.ascontiguousarray(cos.T)
    sinT = np.ascontiguousarray(sin.T).copy()
    sinT[64:, :] *= -1.0
    rows = own_rows(half)
    mask = np.zeros((NSLOT, 8, 128, 512), np.uint8)
    for si in range(NSLOT):
        nkt = 4 * EXT[si]
        gq = rows[si * 512:(si + 1) * 512]
        for mi in range(8):
            kt = nkt - 8 + mi
            kpos = kt * 128 + np.arange(128)
            mask[si, mi] = (kpos[:, None] <= gq[None, :]).astype(np.uint8)
    return {
        "rows": rows,
        "cosq": np.ascontiguousarray(cosT[:, rows]),
        "sinq": np.ascontiguousarray(sinT[:, rows]),
        "cosk": cosT,
        "sink": np.ascontiguousarray(sinT),
        "masks": np.ascontiguousarray(mask.transpose(2, 0, 1, 3)),
    }


def make_in_map(x_full, half, b, wqT, wkT, wvT, woT, gw, uw, dw, gsv, usv, dsv,
                identb):
    hd = host_inputs(half)
    x_b = np.ascontiguousarray(x_full[b])
    return hd["rows"], {
        "xb": x_b,
        "xo": np.ascontiguousarray(x_b[hd["rows"]]),
        "wqT": wqT, "wkT": wkT, "wvT": wvT, "woT": woT,
        "gw": gw, "uw": uw, "dw": dw,
        "gs": gsv, "us": usv, "ds": dsv,
        "cosk": hd["cosk"], "sink": hd["sink"],
        "cosq": hd["cosq"], "sinq": hd["sinq"],
        "identb": identb, "masks": hd["masks"],
    }


def kernel(L0_attn_x, L0_attn_wq, L0_attn_wk, L0_attn_wv, L0_attn_wo,
           L0_ffn_gate_w, L0_ffn_up_w, L0_ffn_down_w,
           L0_ffn_gate_scale, L0_ffn_up_scale, L0_ffn_down_scale):
    nc = build_nc()
    wqT = np.ascontiguousarray(L0_attn_wq.T)
    wkT = np.ascontiguousarray(L0_attn_wk.T)
    wvT = np.ascontiguousarray(L0_attn_wv.T)
    woT = np.ascontiguousarray(L0_attn_wo.T)
    identb = np.eye(128, dtype=ml_dtypes.bfloat16)
    in_maps, rows = [], {}
    for c in range(NCORE):
        b, half = c // 2, c % 2
        r, im = make_in_map(L0_attn_x, half, b, wqT, wkT, wvT, woT,
                            L0_ffn_gate_w, L0_ffn_up_w, L0_ffn_down_w,
                            L0_ffn_gate_scale, L0_ffn_up_scale,
                            L0_ffn_down_scale, identb)
        rows[half] = r
        in_maps.append(im)
    res = run_bass_kernel_spmd(nc, in_maps, core_ids=list(range(NCORE)))
    out = np.empty((B, T, E), np.float32)
    for c in range(NCORE):
        b, half = c // 2, c % 2
        out[b, rows[half]] = res.results[c]["y"]
    return out
